# version 23
# baseline (speedup 1.0000x reference)
import math
from contextlib import ExitStack

import numpy as np
import ml_dtypes

import concourse.bass as bass
import concourse.mybir as mybir
from concourse.bass_utils import run_bass_kernel_spmd

F32 = mybir.dt.float32
BF16 = mybir.dt.bfloat16
AF = mybir.ActivationFunctionType
ALU = mybir.AluOpType
AX = mybir.AxisListType

S, D = 2048, 4096
NH, NKV, HD = 32, 8, 128
GQ = NH // NKV
IC = D // 128
SBn = S // 128
OSLn = D // 512
ISCALE = 1.0 / math.sqrt(HD)

ENGS = ("tensor", "vector", "scalar", "sync", "gpsimd")


class PB:

    def __init__(self, sems):
        self.sems = sems
        self.ctr = {k: 0 for k in sems}
        self.ev = {}
        self.ops = {e: [] for e in ENGS}
        self.pending = {e: [] for e in ENGS}

    def emit(self, eng, fn, waits=(), inc=None, inc_k=1, key=None):
        w = []
        for x in waits:
            if x is None:
                continue
            if isinstance(x, list):
                w += [y for y in x if y is not None]
            else:
                w.append(x)
        if self.pending[eng]:
            w = self.pending[eng] + w
            self.pending[eng] = []
        best = {}
        for (sname, val) in w:
            if best.get(sname, -1) < val:
                best[sname] = val
        w = list(best.items())
        if inc is not None:
            self.ctr[inc] += inc_k
            if key is not None:
                self.ev[key] = (inc, self.ctr[inc])
            self.ops[eng].append((fn, tuple(w), (inc, inc_k)))
        else:
            self.ops[eng].append((fn, tuple(w), None))

    def barrier(self):
        bw = [(k, v) for k, v in self.ctr.items() if v > 0]
        for e in ENGS:
            self.pending[e] = self.pending[e] + bw

    def flush(self, block):
        sems = self.sems

        def runner(engname):
            oplist = self.ops[engname]

            def run(eng):
                for fn, waits, inc in oplist:
                    for (sname, val) in waits:
                        eng.wait_ge(sems[sname], val)
                    inst = fn(eng)
                    if inc is not None:
                        inst.then_inc(sems[inc[0]], inc[1])
            return run

        block.tensor(runner("tensor"))
        block.vector(runner("vector"))
        block.scalar(runner("scalar"))
        block.sync(runner("sync"))
        block.gpsimd(runner("gpsimd"))


def build_nc(debug_taps=False):
    nc = bass.Bass("TRN2", target_bir_lowering=False, debug=False)
    dt = nc.dram_tensor
    i_xt = dt("xt", [IC, 128, S], F32, kind="ExternalInput").ap()
    i_xn = dt("xn", [S, D], F32, kind="ExternalInput").ap()
    i_wq = dt("wq_s", [NH, 128, D], F32, kind="ExternalInput").ap()
    i_wk = dt("wk_s", [NKV, 128, D], F32, kind="ExternalInput").ap()
    i_wv = dt("wv_s", [2, IC, 128, 512], F32, kind="ExternalInput").ap()
    i_wo = dt("wo_s", [OSLn, IC, 128, 512], F32, kind="ExternalInput").ap()
    i_wf = dt("wf_s", [OSLn, IC, 128, 512], F32, kind="ExternalInput").ap()
    i_sp = dt("sp", [1, D], F32, kind="ExternalInput").ap()
    i_cos = dt("ropecos", [128, S], F32, kind="ExternalInput").ap()
    i_sin2 = dt("ropesin2", [128, S], F32, kind="ExternalInput").ap()
    i_tri = dt("trimask", [128, 128], F32, kind="ExternalInput").ap()
    i_tri2 = dt("trimask2", [128, 128], F32, kind="ExternalInput").ap()
    i_eye = dt("eye128", [128, 128], F32, kind="ExternalInput").ap()
    o_out = dt("out", [S, D], F32, kind="ExternalOutput").ap()
    d_kt = dt("kt_scr", [NKV, 128, S], BF16, kind="Internal").ap()
    d_qt = dt("qt_scr", [NH, 128, S], BF16, kind="Internal").ap()
    d_v = dt("v_scr", [SBn, 128, 1024], BF16, kind="Internal").ap()
    d_at = dt("attn_scr", [NH, 128, S], BF16, kind="Internal").ap()
    d_xa = dt("xa_scr", [SBn, 128, D], F32, kind="Internal").ap()
    d_sg = dt("sgn_scr", [OSLn, 128, IC * 512], BF16, kind="Internal").ap()
    d_yf = dt("yf_scr", [SBn, 128, D], F32, kind="Internal").ap()
    d_cs = dt("cs_scr", [1, D], F32, kind="Internal").ap()

    glob = ExitStack()
    en = glob.enter_context
    semnames = (["dinig", "dinis", "dspl", "dwv", "dkv", "dah", "dxab", "dxab1",
                 "dcst", "dcsl", "m", "ve", "se"]
                + [f"dxt{j}" for j in range(4)]
                + [f"dsgc{j}" for j in range(4)]
                + [f"dws{j}" for j in range(2)] + [f"dsp{j}" for j in range(3)]
                + [f"dvs{j}" for j in range(2)] + [f"dqt{j}" for j in range(2)]
                + [f"dat{j}" for j in range(2)] + [f"dwo{j}" for j in range(2)]
                + [f"dxn{j}" for j in range(4)] + [f"dxa{j}" for j in range(2)]
                + [f"dsg{j}" for j in range(2)] + [f"dwc{j}" for j in range(4)]
                + [f"dxc{j}" for j in range(2)] + [f"dyf{j}" for j in range(2)]
                + [f"dyl{j}" for j in range(2)] + [f"dout{j}" for j in range(2)])
    sems = {n: en(nc.semaphore(n)) for n in semnames}
    pb = PB(sems)

    TRI = en(nc.sbuf_tensor("tri", [128, 128], F32))
    EYEB = en(nc.sbuf_tensor("eyeb", [128, 128], BF16))
    ONESB = en(nc.sbuf_tensor("onesb", [128, 1], BF16))
    ONESR = en(nc.sbuf_tensor("onesr", [1, 128], F32))
    SPSL = en(nc.sbuf_tensor("spsl", [1, 512], F32))
    SUM1 = en(nc.sbuf_tensor("sum1", [128, 128], F32))
    SQ1 = en(nc.sbuf_tensor("sq1", [128, 128], F32))
    M1 = en(nc.sbuf_tensor("m1", [128, SBn], F32))
    R1 = en(nc.sbuf_tensor("r1", [128, SBn], F32))
    SUM2 = en(nc.sbuf_tensor("sum2", [128, 128], F32))
    SQ2 = en(nc.sbuf_tensor("sq2", [128, 128], F32))
    M2 = en(nc.sbuf_tensor("m2", [128, SBn], F32))
    R2 = en(nc.sbuf_tensor("r2", [128, SBn], F32))
    TCOL = en(nc.sbuf_tensor("tcol", [128, 8], F32))
    STCOL = en(nc.sbuf_tensor("stcol", [128, 64], F32))
    DTCOL = en(nc.sbuf_tensor("dtcol", [128, 64], F32))
    CSROW = en(nc.sbuf_tensor("csrow", [1, 512], F32))
    ONE128 = en(nc.sbuf_tensor("one128", [128, 128], BF16))
    TRIMB = en(nc.sbuf_tensor("trimb", [128, 128], BF16))
    PS_S0 = en(nc.psum_tensor("ps_s0", [128, 1024], F32))
    PS_S1 = en(nc.psum_tensor("ps_s1", [128, 1024], F32))
    PS_A = en(nc.psum_tensor("ps_a", [128, 512], F32))
    PS_B = en(nc.psum_tensor("ps_b", [128, 512], F32))
    PS_C = en(nc.psum_tensor("ps_c", [128, 512], F32))
    PS_D = en(nc.psum_tensor("ps_d", [128, 512], F32))
    PS4 = [PS_A, PS_B, PS_C, PS_D]
    PSAB = [PS_A, PS_B]

    E = pb.emit
    W = lambda k: pb.ev[k]

    E("gpsimd", lambda g, o=EYEB[:], i=i_eye: g.dma_start(out=o, in_=i),
      inc="dinig", inc_k=16)
    E("gpsimd", lambda g, o=TRIMB[:], i=i_tri2: g.dma_start(out=o, in_=i),
      inc="dinig", inc_k=16)
    E("sync", lambda s, o=TRI[:], i=i_tri: s.dma_start(out=o, in_=i),
      inc="dinis", inc_k=16)
    E("vector", lambda v, o=ONESB[:]: v.memset(o, 1.0), inc="ve", key=("onesb",))
    E("vector", lambda v, o=ONESR[:]: v.memset(o, 1.0), inc="ve", key=("onesr",))
    E("vector", lambda v, o=ONE128[:]: v.memset(o, 1.0), inc="ve",
      key=("ones128",))

    sa = ExitStack()
    ea = sa.enter_context
    XT = ea(nc.sbuf_tensor("xtb", [128, IC * S], BF16))
    WS = [ea(nc.sbuf_tensor(f"ws{j}", [128, D], BF16)) for j in range(2)]
    WV = ea(nc.sbuf_tensor("wvb", [128, IC * 256], BF16))
    COS = ea(nc.sbuf_tensor("cosb", [128, S], F32))
    SIN2 = ea(nc.sbuf_tensor("sin2b", [128, S], F32))
    TMP1 = ea(nc.sbuf_tensor("tmp1", [128, 512], F32))
    TMP2 = ea(nc.sbuf_tensor("tmp2", [128, 512], F32))
    PESP = [ea(nc.sbuf_tensor(f"pesp{j}", [128, 512], BF16)) for j in range(3)]
    VSS = [ea(nc.sbuf_tensor(f"vss{j}", [128, 256], BF16)) for j in range(2)]
    WCH = [ea(nc.sbuf_tensor(f"wch{j}", [128, 512], BF16)) for j in range(4)]
    WABS = [ea(nc.sbuf_tensor(f"wabs{j}", [128, 512], BF16)) for j in range(4)]
    SGNC = [ea(nc.sbuf_tensor(f"sgnc{j}", [128, 512], BF16)) for j in range(4)]

    E("gpsimd", lambda g, o=WS[0][:], i=i_wk[0]: g.dma_start(out=o, in_=i),
      inc="dws0", inc_k=16, key=("k", "w", 0))
    for c in range(IC):
        xw = [W(("xt", c - 4))] if c >= 4 else []
        E("gpsimd", lambda g, o=XT[:, c * S:(c + 1) * S], i=i_xt[c]:
          g.dma_start(out=o, in_=i), waits=xw,
          inc=f"dxt{c % 4}", inc_k=16, key=("xt", c))
    E("sync", lambda s, o=COS[:], i=i_cos: s.dma_start(out=o, in_=i),
      inc="dinis", inc_k=16)
    E("sync", lambda s, o=SIN2[:], i=i_sin2: s.dma_start(out=o, in_=i),
      inc="dinis", inc_k=16)
    pb.ev[("eye",)] = ("dinig", 32)
    pb.ev[("trimb",)] = ("dinig", 32)
    for _k in [("tri",), ("cos",), ("sin",)]:
        pb.ev[_k] = ("dinis", 48)
    xt_all = [("dxt0", 16 * 8), ("dxt1", 16 * 8), ("dxt2", 16 * 8),
              ("dxt3", 16 * 8)]
    rope_rdy = [("dinis", 48)]

    ps_last = [[] for _ in range(4)]
    ws_last = [[] for _ in range(2)]
    pesp_last = [[] for _ in range(3)]
    vss_last = [[] for _ in range(2)]
    st_ctr = {"ps": 0, "pesp": 0, "vss": 0}

    def rope_evac(psum, st, dst_ap, waits, key):
        c0 = st * 512
        E("vector", lambda v, o=TMP1[0:64, :], a=psum[64:128, :],
          b=SIN2[0:64, c0:c0 + 512]: v.tensor_tensor(o, a, b, op=ALU.mult),
          waits=list(waits) + rope_rdy, inc="ve")
        E("vector", lambda v, o=TMP1[64:128, :], a=psum[0:64, :],
          b=SIN2[64:128, c0:c0 + 512]: v.tensor_tensor(o, a, b, op=ALU.mult),
          inc="ve")
        E("vector", lambda v, o=TMP2[:], a=psum[:], b=COS[:, c0:c0 + 512]:
          v.tensor_tensor(o, a, b, op=ALU.mult), inc="ve", key=key + ("t2",))
        E("vector", lambda v, o=dst_ap, a=TMP1[:], b=TMP2[:]:
          v.tensor_add(o, a, b), waits=[W(key + ("t2",))], inc="ve", key=key)

    wch_last = [[] for _ in range(4)]
    wabs_last = [[] for _ in range(4)]
    sgnc_last = [[] for _ in range(4)]
    psc_last = []
    sign_dma = []
    sign_cmp = []

    def mk_sign_dma(osl, c):
        def run():
            wsl = (osl * IC + c) % 4
            E("gpsimd", lambda g, o=WCH[wsl][:], i=i_wf[osl][c]:
              g.dma_start(out=o, in_=i),
              waits=[wch_last[wsl]], inc=f"dwc{wsl}", inc_k=16,
              key=("c", "wch", osl, c))
        return run

    def mk_sign_cmp(osl, c):
        def run():
            wsl = (osl * IC + c) % 4
            asl = (osl * IC + c) % 4
            E("scalar", lambda s, o=WABS[asl][:], i_=WCH[wsl][:]:
              s.activation(o, i_, AF.Abs),
              waits=[W(("c", "wch", osl, c))] + wabs_last[asl],
              inc="se", key=("c", "abs", osl, c))
            E("scalar", lambda s, o=SGNC[asl][:], i_=WCH[wsl][:]:
              s.activation(o, i_, AF.Sign),
              waits=sgnc_last[asl], inc="se", key=("c", "sgn", osl, c))
            wch_last[wsl] = [W(("c", "sgn", osl, c))]
            E("tensor", lambda t, o=PS_S1[:, 0:512], l=ONE128[:],
              rr=WABS[asl][:], s0=(c == 0), s1=(c == IC - 1):
              t.matmul(o, l, rr, start=s0, stop=s1),
              waits=[W(("c", "abs", osl, c)), W(("ones128",))]
              + (psc_last if c == 0 else []),
              inc="m", key=("c", "rsc", osl, c))
            wabs_last[asl] = [W(("c", "rsc", osl, c))]
            E("gpsimd", lambda g, o=d_sg[osl][:, c * 512:(c + 1) * 512],
              i=SGNC[asl][:]: g.dma_start(out=o, in_=i),
              waits=[W(("c", "sgn", osl, c))],
              inc=f"dsgc{asl}", inc_k=16, key=("c", "sgst", osl, c))
            sgnc_last[asl] = [W(("c", "sgst", osl, c))]
            if c == IC - 1:
                spw = [W(("c", "csr", osl - 1))] if osl >= 1 else []
                E("sync", lambda s, o=SPSL[:],
                  i=i_sp[:, osl * 512:(osl + 1) * 512]:
                  s.dma_start(out=o, in_=i), waits=spw,
                  inc="dspl", inc_k=16, key=("c", "spl", osl))
                cw = [W(("c", "rsc", osl, c))]
                if osl >= 1:
                    cw.append(W(("c", "csst", osl - 1)))
                E("vector", lambda v, o=CSROW[:], i0=PS_S1[0:1, 0:512]:
                  v.tensor_scalar(o, i0, 1.0 / D, 1e-8, op0=ALU.mult,
                                  op1=ALU.max),
                  waits=cw, inc="ve",
                  key=("c", "cs0", osl))
                psc_last[:] = [W(("c", "cs0", osl))]
                E("vector", lambda v, o=CSROW[:],
                  a=CSROW[:], b_=SPSL[:]:
                  v.tensor_tensor(o, a, b_, op=ALU.mult),
                  waits=[W(("c", "cs0", osl)), W(("c", "spl", osl))], inc="ve",
                  key=("c", "csr", osl))
                E("sync", lambda s, o=d_cs[:, osl * 512:(osl + 1) * 512],
                  i=CSROW[:]: s.dma_start(out=o, in_=i),
                  waits=[W(("c", "csr", osl))],
                  inc="dcst", inc_k=16, key=("c", "csst", osl))
        return run

    for osl in range(OSLn):
        for c in range(IC):
            sign_dma.append(mk_sign_dma(osl, c))
            sign_cmp.append(mk_sign_cmp(osl, c))
    sign_st = {"d": 0, "c": 0}

    def pop_sign(n_dma, n_cmp):
        for _ in range(n_cmp):
            if sign_st["c"] < sign_st["d"] - 2 and sign_st["c"] < 256:
                sign_cmp[sign_st["c"]]()
                sign_st["c"] += 1
        for _ in range(n_dma):
            if sign_st["d"] < 256:
                sign_dma[sign_st["d"]]()
                sign_st["d"] += 1

    def proj_T(nheads, wsrc, dst_scr, tag):
        for ob in range(nheads):
            sl = ob % 2
            strip = WS[sl]
            if not (tag == "k" and ob == 0):
                E("gpsimd", lambda g, o=strip[:], i=wsrc[ob]: g.dma_start(out=o, in_=i),
                  waits=[ws_last[sl]], inc=f"dws{sl}", inc_k=16, key=(tag, "w", ob))
            last_mm_key = None
            for st in range(4):
                if tag == "q":
                    pop_sign(2, 2)
                psl = st_ctr["ps"] % 4
                st_ctr["ps"] += 1
                psum = PS4[psl]
                chunked = (tag == "k" and ob < 2)
                waits = [W((tag, "w", ob)), ps_last[psl]]
                if not chunked:
                    waits.append(xt_all)
                for c in range(IC):
                    cw = waits if c == 0 else []
                    if chunked:
                        cw = cw + [W(("xt", c))]
                    E("tensor",
                      lambda t, o=psum[:], l=strip[:, c * 128:(c + 1) * 128],
                      r=XT[:, c * S + st * 512: c * S + st * 512 + 512],
                      s0=(c == 0), s1=(c == IC - 1):
                      t.matmul(o, l, r, start=s0, stop=s1),
                      waits=cw,
                      inc=("m" if c == IC - 1 else None),
                      key=((tag, "mm", ob, st) if c == IC - 1 else None))
                last_mm_key = (tag, "mm", ob, st)
                spl = st_ctr["pesp"] % 3
                st_ctr["pesp"] += 1
                sp = PESP[spl]
                rope_evac(psum, st, sp[:], [W(last_mm_key)] + pesp_last[spl],
                          (tag, "rope", ob, st))
                ps_last[psl] = [W((tag, "rope", ob, st))]
                E("sync", lambda s, o=dst_scr[ob][:, st * 512:(st + 1) * 512],
                  i=sp[:]: s.dma_start(out=o, in_=i),
                  waits=[W((tag, "rope", ob, st))],
                  inc=f"dsp{spl}", inc_k=16, key=(tag, "spill", ob, st))
                pesp_last[spl] = [W((tag, "spill", ob, st))]
            ws_last[sl] = [W(last_mm_key)]

    proj_T(NKV, i_wk, d_kt, "k")

    wv_last = []
    for og in range(4):
        h, off = og // 2, (og % 2) * 256
        E("gpsimd", lambda g, o=WV[:].rearrange("p (c o) -> p c o", c=IC),
          i=i_wv[h][:, :, off:off + 256].rearrange("c p o -> p c o"):
          g.dma_start(out=o, in_=i),
          waits=[wv_last], inc="dwv", inc_k=16, key=("v", "w", og))
        last_key = None
        for sb in range(SBn):
            psl = st_ctr["ps"] % 4
            st_ctr["ps"] += 1
            psum = PS4[psl]
            waits = [W(("v", "w", og)), ps_last[psl], xt_all]
            for c in range(IC):
                E("tensor",
                  lambda t, o=psum[:, 0:256],
                  l=XT[:, c * S + sb * 128: c * S + sb * 128 + 128],
                  r=WV[:, c * 256:(c + 1) * 256],
                  s0=(c == 0), s1=(c == IC - 1):
                  t.matmul(o, l, r, start=s0, stop=s1),
                  waits=(waits if c == 0 else ()),
                  inc=("m" if c == IC - 1 else None),
                  key=(("v", "mm", og, sb) if c == IC - 1 else None))
            last_key = ("v", "mm", og, sb)
            vsl = st_ctr["vss"] % 2
            st_ctr["vss"] += 1
            vs = VSS[vsl]
            E("scalar", lambda s, o=vs[:], i=psum[:, 0:256]: s.copy(o, i),
              waits=[W(("v", "mm", og, sb))] + vss_last[vsl],
              inc="se", key=("v", "ev", og, sb))
            ps_last[psl] = [W(("v", "ev", og, sb))]
            E("sync", lambda s, o=d_v[sb][:, og * 256:og * 256 + 256], i=vs[:]:
              s.dma_start(out=o, in_=i),
              waits=[W(("v", "ev", og, sb))], inc=f"dvs{vsl}", inc_k=16,
              key=("v", "spill", og, sb))
            vss_last[vsl] = [W(("v", "spill", og, sb))]
        wv_last = [W(last_key)]

    proj_T(NH, i_wq, d_qt, "q")
    while sign_st["c"] < 256:
        sign_cmp[sign_st["c"]]()
        sign_st["c"] += 1

    pb.barrier()
    sa.close()

    sb_ = ExitStack()
    eb = sb_.enter_context
    KT = eb(nc.sbuf_tensor("ktb", [128, NKV * S], BF16))
    VSB = eb(nc.sbuf_tensor("vsbt", [128, SBn * 1024], BF16))
    QT = [eb(nc.sbuf_tensor(f"qtb{j}", [128, S], BF16)) for j in range(2)]
    PRBP = [eb(nc.sbuf_tensor(f"prbp{j}", [128, 1024], BF16)) for j in range(3)]
    LNT = [eb(nc.sbuf_tensor(f"lnt{j}", [128, 512], F32)) for j in range(2)]
    RDB = [eb(nc.sbuf_tensor(f"rdb{j}", [128, 512], F32)) for j in range(2)]
    ATTN = [eb(nc.sbuf_tensor(f"attn{j}", [128, S], BF16)) for j in range(2)]

    E("sync", lambda s, o=KT[:].rearrange("p (o s) -> p o s", o=NKV),
      i=d_kt.rearrange("o p s -> p o s"):
      s.dma_start(out=o, in_=i), inc="dkv", inc_k=16, key=("B", "kt"))
    E("sync", lambda s, o=VSB[:].rearrange("p (j o) -> p j o", j=SBn),
      i=d_v.rearrange("j p o -> p j o"):
      s.dma_start(out=o, in_=i), inc="dkv", inc_k=16, key=("B", "vl"))
    pb.ev[("B", "kt")] = ("dkv", 32)
    pb.ev[("B", "vl")] = ("dkv", 32)
    E("sync", lambda s, o=QT[0][:], i=d_qt[0]: s.dma_start(out=o, in_=i),
      inc="dqt0", inc_k=16, key=("B", "qt", 0))

    PSP = [PS_S0, PS_S1]

    def den_ap(dsl, c0=0):
        return (PS_C if dsl == 0 else PS_D)[:, c0:512]

    def pv_ap(pvsl, c0=0):
        return (PS_A if pvsl == 0 else PS_B)[:, c0:512]

    sc_last = [[], []]
    prb_last = [[], [], []]
    pv_last = [[], []]
    den_rdy = [[], []]
    lnt_war = [[], []]
    rdb_war = [[], []]
    attn_last = [[], []]
    bst = {"pr": 0}
    pend_dp = []
    pend_epi = []

    def emit_epi(h, qc, dsl, pvsl):
        def run():
            E("scalar", lambda s, o=LNT[dsl][:], i_=den_ap(dsl):
              s.activation(o, i_, AF.Ln),
              waits=[W(("B", "dn", h, qc))] + lnt_war[dsl],
              inc="se", key=("B", "lnd", h, qc))
            den_rdy[dsl] = [W(("B", "lnd", h, qc))]
            E("scalar", lambda s, o=RDB[dsl][:], i_=LNT[dsl][:]:
              s.activation(o, i_, AF.Exp, scale=-1.0),
              waits=[W(("B", "lnd", h, qc))] + rdb_war[dsl],
              inc="se", key=("B", "rcd", h, qc))
            lnt_war[dsl] = [W(("B", "rcd", h, qc))]
            w = [W(("B", "pv", h, qc)), W(("B", "rcd", h, qc))]
            if qc == 0:
                w += attn_last[h % 2]
            E("vector", lambda v, o=ATTN[h % 2][:, qc * 512:(qc + 1) * 512],
              a=pv_ap(pvsl), b_=RDB[dsl][:]:
              v.tensor_tensor(o, a, b_, op=ALU.mult),
              waits=w, inc="ve", key=("B", "pve", h, qc))
            pv_last[pvsl] = [W(("B", "pve", h, qc))]
            rdb_war[dsl] = [W(("B", "pve", h, qc))]
            if qc == 3:
                E("sync", lambda s, o=d_at[h], i=ATTN[h % 2][:]:
                  s.dma_start(out=o, in_=i),
                  waits=[W(("B", "pve", h, 3))],
                  inc=f"dat{h % 2}", inc_k=16, key=("B", "aspill", h))
                attn_last[h % 2] = [W(("B", "aspill", h))]
        return run

    def emit_dp(h, qc, j, gi, dsl, pvsl, prsl, njs, kv):
        def run():
            c0 = max(0, j - 4 * qc) * 128
            po = (j % 2) * 512
            dw = [W(("B", "exp", h, qc, j // 2)), W(("ones128",))]
            if j == 0:
                dw += den_rdy[dsl]
            E("tensor", lambda t, o=den_ap(dsl, c0), l=ONE128[:],
              rr=PRBP[prsl][:, po + c0: po + 512],
              s0=(j == 0), s1=(j == njs - 1):
              t.matmul(o, l, rr, start=s0, stop=s1),
              waits=dw, inc="m",
              key=(("B", "dn", h, qc) if j == njs - 1 else ("B", "dnj", gi, j)))
            pw = [W(("B", "exp", h, qc, j // 2)), W(("B", "vl"))]
            if j == 0:
                pw += pv_last[pvsl]
            E("tensor", lambda t, o=pv_ap(pvsl, c0),
              l=VSB[:, j * 1024 + kv * 128: j * 1024 + (kv + 1) * 128],
              rr=PRBP[prsl][:, po + c0: po + 512],
              s0=(j == 0), s1=(j == njs - 1):
              t.matmul(o, l, rr, start=s0, stop=s1),
              waits=pw, inc="m",
              key=(("B", "pv", h, qc) if j == njs - 1 else ("B", "pvj", gi, j)))
            if j % 2 == 1:
                prb_last[prsl] = [W(("B", "pv", h, qc) if j == njs - 1
                                    else ("B", "pvj", gi, j))]
        return run

    for h in range(NH):
        kv = h // GQ
        if h + 1 < NH:
            qw = [W(("B", "sc", h - 1, 3, 15))] if h >= 1 else []
            E("sync", lambda s, o=QT[(h + 1) % 2][:], i=d_qt[h + 1]:
              s.dma_start(out=o, in_=i), waits=qw,
              inc=f"dqt{(h + 1) % 2}", inc_k=16, key=("B", "qt", h + 1))
        for qc in range(4):
            gi = h * 4 + qc
            njs = 4 * qc + 4
            dsl = gi % 2
            pvsl = gi % 2
            for j in range(njs):
                c0 = max(0, j - 4 * qc) * 128
                pr = bst["pr"]
                ssl = pr % 2
                prsl = pr % 3
                po = (j % 2) * 512
                scw = [W(("B", "qt", h)), W(("B", "kt"))]
                if j % 2 == 0:
                    scw = scw + sc_last[ssl]
                E("tensor",
                  lambda t, o=PSP[ssl][:, po + c0: po + 512],
                  l=KT[:, kv * S + j * 128: kv * S + (j + 1) * 128],
                  rr=QT[h % 2][:, qc * 512 + c0: (qc + 1) * 512]:
                  t.matmul(o, l, rr, start=True, stop=True),
                  waits=scw, inc="m", key=("B", "sc", h, qc, j))
                if j >= 4 * qc:
                    E("tensor",
                      lambda t, o=PSP[ssl][:, po + c0: po + c0 + 128],
                      l=TRIMB[:], rr=EYEB[:]:
                      t.matmul(o, l, rr, start=False, stop=True),
                      waits=[W(("trimb",)), W(("eye",))],
                      inc="m", key=("B", "mk", h, qc, j))
                if j % 2 == 1:
                    c0e = max(0, (j - 1) - 4 * qc) * 128
                    if j >= 4 * qc:
                        expw = [W(("B", "mk", h, qc, j))]
                    else:
                        expw = [W(("B", "sc", h, qc, j))]
                    E("scalar", lambda s, o=PRBP[prsl][:, c0e:1024],
                      i_=PSP[ssl][:, c0e:1024]:
                      s.activation(o, i_, AF.Exp, scale=ISCALE),
                      waits=expw + prb_last[prsl], inc="se",
                      key=("B", "exp", h, qc, j // 2))
                    sc_last[ssl] = [W(("B", "exp", h, qc, j // 2))]
                    bst["pr"] += 1
                    pend_dp.append(emit_dp(h, qc, j - 1, gi, dsl, pvsl, prsl,
                                           njs, kv))
                    pend_dp.append(emit_dp(h, qc, j, gi, dsl, pvsl, prsl,
                                           njs, kv))
                    if len(pend_dp) > 4:
                        pend_dp.pop(0)()
                        pend_dp.pop(0)()
                if j == 3 and pend_epi:
                    pend_epi.pop(0)()
            pend_epi.append(emit_epi(h, qc, dsl, pvsl))
    while pend_dp:
        pend_dp.pop(0)()
    while pend_epi:
        pend_epi.pop(0)()

    pb.barrier()
    sb_.close()

    sw = ExitStack()
    ew_ = sw.enter_context
    ATTNH = ew_(nc.sbuf_tensor("attnh", [128, IC * 1024], BF16))
    WOS = [ew_(nc.sbuf_tensor(f"wos{j}", [128, IC * 512], BF16)) for j in range(2)]
    XNT = [ew_(nc.sbuf_tensor(f"xnt{j}", [128, 512], F32)) for j in range(4)]
    XAT = [ew_(nc.sbuf_tensor(f"xat{j}", [128, 512], F32)) for j in range(4)]
    SQT = ew_(nc.sbuf_tensor("sqtb", [128, 512], BF16))

    def psw(slot):
        if slot == 0:
            return PS_A[:]
        if slot == 1:
            return PS_B[:]
        if slot == 2:
            return PS_S0[:, 0:512]
        if slot == 3:
            return PS_S0[:, 512:1024]
        if slot == 4:
            return PS_C[:]
        if slot == 5:
            return PS_D[:]
        if slot == 6:
            return PS_S1[:, 0:512]
        return PS_S1[:, 512:1024]

    wos_last = [[], []]
    xaspill_prev = [[], []]
    xnt_last = [[] for _ in range(4)]
    xat_last = [[] for _ in range(4)]
    psw_last = [[] for _ in range(8)]
    attnh_last = []

    for half in range(2):
        for osl in range(2):
            E("gpsimd", lambda g, o=WOS[osl % 2][:].rearrange("p (c o) -> p c o", c=IC),
              i=i_wo[osl].rearrange("c p o -> p c o"): g.dma_start(out=o, in_=i),
              waits=[wos_last[osl % 2]], inc=f"dwo{osl % 2}", inc_k=16,
              key=("w1", "w", half, osl))
        E("sync", lambda s, o=ATTNH[:, 0:16 * 1024].rearrange("p (c s) -> p c s", c=16),
          i=d_at[0:16, :, half * 1024:(half + 1) * 1024].rearrange("c p s -> p c s"):
          s.dma_start(out=o, in_=i), waits=[attnh_last],
          inc="dah", inc_k=16, key=("w1", "ah_a", half))
        E("gpsimd", lambda g, o=ATTNH[:, 16 * 1024:32 * 1024].rearrange("p (c s) -> p c s", c=16),
          i=d_at[16:32, :, half * 1024:(half + 1) * 1024].rearrange("c p s -> p c s"):
          g.dma_start(out=o, in_=i), waits=[attnh_last],
          inc="dah", inc_k=16, key=("w1", "ah_b", half))
        pb.ev[("w1", "ah", half)] = ("dah", 32 * (half + 1))
        for osl in range(OSLn):
            if osl >= 2:
                E("gpsimd", lambda g, o=WOS[osl % 2][:].rearrange("p (c o) -> p c o", c=IC),
                  i=i_wo[osl].rearrange("c p o -> p c o"): g.dma_start(out=o, in_=i),
                  waits=[wos_last[osl % 2]], inc=f"dwo{osl % 2}", inc_k=16,
                  key=("w1", "w", half, osl))
            for sbl in range(8):
                sb = half * 8 + sbl
                idx = osl * 8 + sbl
                psl = idx % 8
                col = sb * 8 + osl
                xsl = idx % 4
                E("sync", lambda s, o=XNT[xsl][:],
                  i=i_xn[sb * 128:(sb + 1) * 128, osl * 512:(osl + 1) * 512]:
                  s.dma_start(out=o, in_=i), waits=[xnt_last[xsl]],
                  inc=f"dxn{xsl}", inc_k=16, key=("w1", "xn", half, idx))
                mmw = [W(("w1", "w", half, osl)), W(("w1", "ah", half))] + psw_last[psl]
                for c in range(IC):
                    E("tensor",
                      lambda t, o=psw(psl),
                      l=ATTNH[:, c * 1024 + sbl * 128: c * 1024 + (sbl + 1) * 128],
                      r=WOS[osl % 2][:, c * 512:(c + 1) * 512],
                      s0=(c == 0), s1=(c == IC - 1):
                      t.matmul(o, l, r, start=s0, stop=s1),
                      waits=(mmw if c == 0 else ()),
                      inc=("m" if c == IC - 1 else None),
                      key=(("w1", "mm", half, osl, sbl) if c == IC - 1 else None))
                xa = XAT[idx % 4]
                E("vector", lambda v, o=xa[:], a=psw(psl), b_=XNT[xsl][:]:
                  v.tensor_add(o, a, b_),
                  waits=[W(("w1", "mm", half, osl, sbl)), W(("w1", "xn", half, idx))]
                  + xat_last[idx % 4],
                  inc="ve", key=("w1", "xa", half, idx))
                psw_last[psl] = [W(("w1", "xa", half, idx))]
                xnt_last[xsl] = [W(("w1", "xa", half, idx))]
                E("vector", lambda v, o=SUM1[:, col:col + 1], i_=xa[:]:
                  v.reduce_sum(o, i_, axis=AX.X), inc="ve",
                  key=(("w1", "s1w", sb) if osl == OSLn - 1 else None))
                E("scalar", lambda s, o=SQT[:], i_=xa[:], ac=SQ1[:, col:col + 1]:
                  s.activation(o, i_, AF.Square, accum_out=ac),
                  waits=[W(("w1", "xa", half, idx))],
                  inc="se", key=("w1", "sq", sb, osl))
                E("sync", lambda s, o=d_xa[sb][:, osl * 512:(osl + 1) * 512], i=xa[:]:
                  s.dma_start(out=o, in_=i),
                  waits=[W(("w1", "xa", half, idx)), xaspill_prev[sbl % 2]],
                  inc=f"dxa{sbl % 2}", inc_k=16, key=("w1", "xast", sb, osl))
                xaspill_prev[sbl % 2] = [W(("w1", "xast", sb, osl))]
                xat_last[idx % 4] = [W(("w1", "xast", sb, osl)), W(("w1", "sq", sb, osl))]
            wos_last[osl % 2] = [W(("w1", "mm", half, osl, 7))]
        attnh_last = [W(("w1", "mm", half, OSLn - 1, 7))]

    pb.barrier()
    sw.close()

    sc_ = ExitStack()
    ec = sc_.enter_context
    Y1T = ec(nc.sbuf_tensor("y1t", [128, IC * 1024], BF16))
    XABH = [ec(nc.sbuf_tensor(f"xabh{j}", [128, 2048], F32)) for j in range(2)]
    SGN = [ec(nc.sbuf_tensor(f"sgn{j}", [128, IC * 512], BF16)) for j in range(2)]
    CSRA = [ec(nc.sbuf_tensor(f"csra{j}", [1, 512], F32)) for j in range(2)]
    CSBIG = ec(nc.sbuf_tensor("csbig", [128, D], F32))
    Y1C = [ec(nc.sbuf_tensor(f"y1c{j}", [128, 512], BF16)) for j in range(4)]
    DIAGR = [ec(nc.sbuf_tensor(f"diagr{j}", [128, 128], BF16)) for j in range(2)]
    XCT = [ec(nc.sbuf_tensor(f"xct{j}", [128, 512], F32)) for j in range(2)]
    TFB = ec(nc.sbuf_tensor("tfb", [128, 512], F32))
    TYB = ec(nc.sbuf_tensor("tyb", [128, 512], F32))
    TYF = [ec(nc.sbuf_tensor(f"tyf{j}", [128, 512], F32)) for j in range(2)]
    SQT2 = ec(nc.sbuf_tensor("sqt2", [128, 512], BF16))
    YFD = [ec(nc.sbuf_tensor(f"yfd{j}", [128, 1024], F32)) for j in range(2)]
    OUTD = [ec(nc.sbuf_tensor(f"outd{j}", [128, 1024], F32)) for j in range(2)]
    y1t_view = Y1T[:].rearrange("p (c q) -> p c q", c=IC)

    def psr(slot, lo=0, hi=512):
        if slot == 0:
            return PS_A[:, lo:hi]
        if slot == 1:
            return PS_B[:, lo:hi]
        if slot == 2:
            return PS_S0[:, lo:hi]
        if slot == 3:
            return PS_S0[:, 512 + lo:512 + hi]
        if slot == 4:
            return PS_C[:, lo:hi]
        if slot == 5:
            return PS_D[:, lo:hi]
        if slot == 6:
            return PS_S1[:, lo:hi]
        return PS_S1[:, 512 + lo:512 + hi]

    cst = {"ps": 0}
    xabh_last = [[], []]
    y1c_last = [[] for _ in range(4)]
    diagr_last = [[], []]
    psab_last = [[] for _ in range(8)]
    pscd_last = [[], []]
    sgn_last = [[], []]
    xct_last = [[], []]
    tyf_last = [[], []]
    yfd_last = [[], []]
    outd_last = [[], []]
    y1t_war = []

    def emit_csbig():
        for osl in range(OSLn):
            csl = osl % 2
            lw = [W(("c", "csb", osl - 1))] if osl >= 1 else []
            E("sync", lambda s, o=CSRA[csl][:],
              i=d_cs[:, osl * 512:(osl + 1) * 512]: s.dma_start(out=o, in_=i),
              waits=lw, inc="dcsl", inc_k=16, key=("c", "csra", osl))
            E("tensor", lambda t, o=(PS_C if csl == 0 else PS_D)[:],
              l=ONESR[:], rr=CSRA[csl][:]:
              t.matmul(o, l, rr, start=True, stop=True),
              waits=[W(("onesr",)), W(("c", "csra", osl))] + pscd_last[csl],
              inc="m", key=("c", "csb", osl))
            E("vector", lambda v, o=CSBIG[:, osl * 512:(osl + 1) * 512],
              i_=(PS_C if csl == 0 else PS_D)[:]:
              v.tensor_copy(o, i_),
              waits=[W(("c", "csb", osl))],
              inc="ve", key=("c", "csbe", osl))
            pscd_last[csl] = [W(("c", "csbe", osl))]
        psab_last[4] = list(pscd_last[0])
        psab_last[5] = list(pscd_last[1])

    def emit_stats1():
        for sb in range(16):
            tc = lambda k, sb=sb: STCOL[:, sb * 4 + k: sb * 4 + k + 1]
            E("vector", lambda v, o=tc(0), i_=SUM1[:, sb * 8:sb * 8 + 8]:
              v.reduce_sum(o, i_, axis=AX.X),
              waits=[W(("w1", "sq", sb, 7))], inc="ve", key=("b2", "s0", sb))
            E("vector", lambda v, o=M1[:, sb:sb + 1], a=tc(0):
              v.tensor_scalar_mul(o, a, 1.0 / D),
              waits=[W(("b2", "s0", sb))], inc="ve", key=("b2", "s1", sb))
            E("vector", lambda v, o=tc(1), i_=SQ1[:, sb * 8:sb * 8 + 8]:
              v.reduce_sum(o, i_, axis=AX.X),
              inc="ve", key=("b2", "s2", sb))
            E("vector", lambda v, o=tc(1), a=tc(1):
              v.tensor_scalar(o, a, 1.0 / D, 1.0e-5, op0=ALU.mult, op1=ALU.add),
              waits=[W(("b2", "s2", sb))], inc="ve", key=("b2", "s3", sb))
            E("vector", lambda v, o=tc(2), a=M1[:, sb:sb + 1], b_=M1[:, sb:sb + 1]:
              v.tensor_tensor(o, a, b_, op=ALU.mult),
              waits=[W(("b2", "s1", sb))], inc="ve", key=("b2", "s4", sb))
            E("vector", lambda v, o=tc(1), a=tc(1), b_=tc(2):
              v.tensor_tensor(o, a, b_, op=ALU.subtract),
              waits=[W(("b2", "s3", sb)), W(("b2", "s4", sb))],
              inc="ve", key=("b2", "var", sb))
            E("scalar", lambda s, o=tc(3), i_=tc(1):
              s.activation(o, i_, AF.Ln),
              waits=[W(("b2", "var", sb))], inc="se", key=("b2", "ln", sb))
            E("scalar", lambda s, o=R1[:, sb:sb + 1], i_=tc(3):
              s.activation(o, i_, AF.Exp, scale=-0.5),
              waits=[W(("b2", "ln", sb))], inc="se", key=("b2", "r1", sb))

    def emit_b2(half):
        nonlocal y1t_war
        for sbl in range(8):
            sb = half * 8 + sbl
            for hi in range(2):
                par = (sb * 2 + hi) % 2
                E("gpsimd", lambda g, o=XABH[par][:],
                  i=d_xa[sb][:, hi * 2048:(hi + 1) * 2048]:
                  g.dma_start(out=o, in_=i),
                  waits=[xabh_last[par], W(("w1", "xast", sb, 7))],
                  inc=("dxab" if par == 0 else "dxab1"), inc_k=16,
                  key=("b2", "xab", sb, hi))
            dsl = sbl % 2
            E("vector", lambda v, o=DIAGR[dsl][:], a=EYEB[:], sc=R1[:, sb:sb + 1]:
              v.tensor_scalar_mul(o, a, sc),
              waits=[W(("b2", "r1", sb)), W(("eye",))] + diagr_last[dsl],
              inc="ve", key=("b2", "dgr", sb))
            for g in range(8):
                hi = g // 4
                par = (sb * 2 + hi) % 2
                lc = g * 512 - hi * 2048
                ysl = g % 4
                E("vector", lambda v, o=Y1C[ysl][:],
                  i0=XABH[par][:, lc:lc + 512], sc=M1[:, sb:sb + 1]:
                  v.tensor_single_scalar(o, i0, sc, op=ALU.subtract),
                  waits=[W(("b2", "xab", sb, hi)), W(("b2", "s1", sb))]
                  + y1c_last[ysl],
                  inc="ve", key=("b2", "y1c", sb, g))
                if g == 3 or g == 7:
                    xabh_last[par] = [W(("b2", "y1c", sb, g))]
                psl = cst["ps"] % 8
                cst["ps"] += 1
                for u in range(4):
                    E("tensor",
                      lambda t, o=psr(psl, u * 128, (u + 1) * 128),
                      l=Y1C[ysl][:, u * 128:(u + 1) * 128], rr=DIAGR[dsl][:]:
                      t.matmul(o, l, rr, start=True, stop=True),
                      waits=([W(("b2", "y1c", sb, g)), W(("b2", "dgr", sb))]
                             + psab_last[psl] if u == 0 else ()),
                      inc=("m" if u == 3 else None),
                      key=(("b2", "tp", sb, g) if u == 3 else None))
                E("scalar",
                  lambda s, o=y1t_view[:, g * 4:(g + 1) * 4, sbl * 128:(sbl + 1) * 128],
                  i_=psr(psl).rearrange("p (u l) -> p u l", u=4):
                  s.copy(o, i_),
                  waits=[W(("b2", "tp", sb, g))] + y1t_war,
                  inc="se", key=("b2", "y1t", sb, g))
                psab_last[psl] = [W(("b2", "y1t", sb, g))]
                y1c_last[ysl] = [W(("b2", "tp", sb, g))]
            diagr_last[dsl] = [W(("b2", "tp", sb, 7))]
        y1t_war = []

    def emit_c(half, gi, sbls, dpops=(), set_war=False):
        nonlocal y1t_war
        dpops = list(dpops)
        ndp = len(dpops)
        npop = 0
        E("gpsimd", lambda g, o=SGN[0][:], i=d_sg[0]: g.dma_start(out=o, in_=i),
          waits=[sgn_last[0]],
          inc="dsg0", inc_k=16, key=("c", "sgld", half, gi, 0))
        for osl in range(OSLn):
            ssl = osl % 2
            if osl + 1 < OSLn:
                nsl = (osl + 1) % 2
                E("gpsimd", lambda g, o=SGN[nsl][:], i=d_sg[osl + 1]:
                  g.dma_start(out=o, in_=i),
                  waits=[sgn_last[nsl]],
                  inc=f"dsg{nsl}", inc_k=16, key=("c", "sgld", half, gi, osl + 1))
            sgn_rdy = W(("c", "sgld", half, gi, osl))
            while npop < (osl + 1) * ndp // OSLn:
                dpops.pop(0)()
                npop += 1
            for sbl in sbls:
                sb = half * 8 + sbl
                idx = osl * 8 + sbl
                psl = cst["ps"] % 8
                cst["ps"] += 1
                col = sb * 8 + osl
                xsl = idx % 2
                E("sync", lambda s, o=XCT[xsl][:],
                  i=d_xa[sb][:, osl * 512:(osl + 1) * 512]:
                  s.dma_start(out=o, in_=i),
                  waits=[xct_last[xsl], W(("w1", "xast", sb, osl))],
                  inc=f"dxc{xsl}", inc_k=16, key=("c", "xct", half, idx))
                for c in range(IC):
                    E("tensor",
                      lambda t, o=psr(psl),
                      l=Y1T[:, c * 1024 + sbl * 128: c * 1024 + (sbl + 1) * 128],
                      r=SGN[ssl][:, c * 512:(c + 1) * 512],
                      s0=(c == 0), s1=(c == IC - 1):
                      t.matmul(o, l, r, start=s0, stop=s1),
                      waits=([sgn_rdy, W(("b2", "y1t", sb, 7))] + psab_last[psl]
                             if c == 0 else ()),
                      inc=("m" if c == IC - 1 else None),
                      key=(("c", "mm", half, osl, sbl) if c == IC - 1 else None))
                E("vector", lambda v, o=TFB[:], a=psr(psl),
                  b_=CSBIG[:, osl * 512:(osl + 1) * 512]:
                  v.tensor_tensor(o, a, b_, op=ALU.mult),
                  waits=[W(("c", "mm", half, osl, sbl)),
                         W(("c", "csbe", osl))],
                  inc="ve", key=("c", "tf", half, idx))
                psab_last[psl] = [W(("c", "tf", half, idx))]
                E("vector", lambda v, o=TFB[:], i0=TFB[:]:
                  v.tensor_scalar(o, i0, 1.0e6, -1.0e6, op0=ALU.min, op1=ALU.max),
                  inc="ve")
                E("vector", lambda v, o=TYB[:], i0=XCT[xsl][:], s1=M1[:, sb:sb + 1],
                  s2=R1[:, sb:sb + 1]:
                  v.tensor_scalar(o, i0, s1, s2, op0=ALU.subtract, op1=ALU.mult),
                  waits=[W(("c", "xct", half, idx)), W(("b2", "r1", sb))],
                  inc="ve", key=("c", "tyb", half, idx))
                tf = TYF[idx % 2]
                E("vector", lambda v, o=tf[:], a=TFB[:], b_=TYB[:]:
                  v.tensor_add(o, a, b_),
                  waits=tyf_last[idx % 2] + [W(("c", "tyb", half, idx))],
                  inc="ve", key=("c", "yf", half, idx))
                xct_last[xsl] = [W(("c", "yf", half, idx))]
                E("vector", lambda v, o=SUM2[:, col:col + 1], i_=tf[:]:
                  v.reduce_sum(o, i_, axis=AX.X), inc="ve",
                  key=(("c", "s2w", sb) if osl == OSLn - 1 else None))
                E("scalar", lambda s, o=SQT2[:], i_=tf[:], ac=SQ2[:, col:col + 1]:
                  s.activation(o, i_, AF.Square, accum_out=ac),
                  waits=[W(("c", "yf", half, idx))],
                  inc="se", key=("c", "sq2", sb, osl))
                E("sync", lambda s, o=d_yf[sb][:, osl * 512:(osl + 1) * 512], i=tf[:]:
                  s.dma_start(out=o, in_=i),
                  waits=[W(("c", "yf", half, idx))],
                  inc=f"dyf{idx % 2}", inc_k=16, key=("c", "yfst", sb, osl))
                tyf_last[idx % 2] = [W(("c", "yfst", sb, osl)), W(("c", "sq2", sb, osl))]
            sgn_last[ssl] = [W(("c", "mm", half, osl, sbls[-1]))]
        while dpops:
            dpops.pop(0)()
        if set_war:
            y1t_war = [W(("c", "mm", half, OSLn - 1, sbls[-1]))]

    def emit_d(half, sbset=tuple(range(8))):
        cls = []
        for sbl_ in sbset:
            def runsb(sbl=sbl_):
                sb = half * 8 + sbl
                tc = lambda k, sb=sb: DTCOL[:, sb * 4 + k: sb * 4 + k + 1]
                E("vector", lambda v, o=tc(0), i_=SUM2[:, sb * 8:sb * 8 + 8]:
                  v.reduce_sum(o, i_, axis=AX.X),
                  waits=[W(("c", "sq2", sb, 7)), W(("c", "s2w", sb)),
                         W(("c", "yf", half, 7 * 8 + sbl))],
                  inc="ve", key=("d", "s0", sb))
                E("vector", lambda v, o=M2[:, sb:sb + 1], a=tc(0):
                  v.tensor_scalar_mul(o, a, 1.0 / D),
                  waits=[W(("d", "s0", sb))], inc="ve", key=("d", "s1", sb))
                E("vector", lambda v, o=tc(1), i_=SQ2[:, sb * 8:sb * 8 + 8]:
                  v.reduce_sum(o, i_, axis=AX.X),
                  inc="ve", key=("d", "s2", sb))
                E("vector", lambda v, o=tc(1), a=tc(1):
                  v.tensor_scalar(o, a, 1.0 / D, 1.0e-5, op0=ALU.mult, op1=ALU.add),
                  waits=[W(("d", "s2", sb))], inc="ve", key=("d", "s3", sb))
                E("vector", lambda v, o=tc(2), a=M2[:, sb:sb + 1],
                  b_=M2[:, sb:sb + 1]:
                  v.tensor_tensor(o, a, b_, op=ALU.mult),
                  waits=[W(("d", "s1", sb))], inc="ve", key=("d", "s4", sb))
                E("vector", lambda v, o=tc(1), a=tc(1), b_=tc(2):
                  v.tensor_tensor(o, a, b_, op=ALU.subtract),
                  waits=[W(("d", "s3", sb)), W(("d", "s4", sb))],
                  inc="ve", key=("d", "var", sb))
                E("scalar", lambda s, o=tc(3), i_=tc(1):
                  s.activation(o, i_, AF.Ln),
                  waits=[W(("d", "var", sb))], inc="se", key=("d", "ln", sb))
                E("scalar", lambda s, o=R2[:, sb:sb + 1], i_=tc(3):
                  s.activation(o, i_, AF.Exp, scale=-0.5),
                  waits=[W(("d", "ln", sb))], inc="se", key=("d", "r2", sb))
                for ch in range(4):
                    idx = sbl * 4 + ch
                    ysl = idx % 2
                    E("gpsimd", lambda g, o=YFD[ysl][:],
                      i=d_yf[sb][:, ch * 1024:(ch + 1) * 1024]:
                      g.dma_start(out=o, in_=i),
                      waits=[yfd_last[ysl], W(("c", "yfst", sb, 2 * ch)),
                             W(("c", "yfst", sb, 2 * ch + 1))],
                      inc=f"dyl{ysl}", inc_k=16, key=("d", "yft", half, idx))
                    E("vector", lambda v, o=OUTD[ysl][:], i0=YFD[ysl][:],
                      s1=M2[:, sb:sb + 1], s2=R2[:, sb:sb + 1]:
                      v.tensor_scalar(o, i0, s1, s2, op0=ALU.subtract, op1=ALU.mult),
                      waits=[W(("d", "yft", half, idx)), W(("d", "r2", sb))]
                      + outd_last[ysl],
                      inc="ve", key=("d", "ot", half, idx))
                    yfd_last[ysl] = [W(("d", "ot", half, idx))]
                    E("sync", lambda s, o=o_out[sb * 128:(sb + 1) * 128,
                                              ch * 1024:(ch + 1) * 1024],
                      i=OUTD[ysl][:]: s.dma_start(out=o, in_=i),
                      waits=[W(("d", "ot", half, idx))],
                      inc=f"dout{ysl}", inc_k=16, key=("d", "ost", half, idx))
                    outd_last[ysl] = [W(("d", "ost", half, idx))]
            cls.append(runsb)
        return cls

    emit_csbig()
    emit_stats1()
    emit_b2(0)
    emit_c(0, 0, list(range(8)), set_war=True)
    emit_b2(1)
    emit_c(1, 0, list(range(4)), dpops=emit_d(0))
    emit_c(1, 1, list(range(4, 8)), dpops=emit_d(1, (0, 1, 2, 3)))
    for fn in emit_d(1, (4, 5, 6, 7)):
        fn()

    if debug_taps:
        taps = [("tap_kt", d_kt, [NKV, 128, S], BF16), ("tap_qt", d_qt, [NH, 128, S], BF16),
                ("tap_v", d_v, [SBn, 128, 1024], BF16), ("tap_at", d_at, [NH, 128, S], BF16),
                ("tap_xa", d_xa, [SBn, 128, D], F32), ("tap_yf", d_yf, [SBn, 128, D], F32),
                ("tap_sg", d_sg, [OSLn, 128, IC * 512], BF16),
                ("tap_cs", d_cs, [1, D], F32)]
        pb.barrier()
        for name, src, shp, dtp in taps:
            o_tap = nc.dram_tensor(name, shp, dtp, kind="ExternalOutput").ap()
            E("sync", lambda s, o=o_tap, i=src: s.dma_start(out=o, in_=i),
              inc="dout0", inc_k=16)

    E("sync", lambda s: s.nop(),
      waits=[("dout0", pb.ctr["dout0"]), ("dout1", pb.ctr["dout1"])])

    with nc.Block() as block:
        pb.flush(block)
    sc_.close()
    glob.close()
    return nc


_NC_CACHE = None


def _host_prep(L0_attn_x, L0_attn_wq, L0_attn_wk, L0_attn_wv, L0_attn_wo,
               L0_ffn_weight, L0_ffn_scale_param):
    wq = np.ascontiguousarray(
        L0_attn_wq.reshape(NH, 128, IC, 128).transpose(0, 3, 2, 1)
    ).reshape(NH, 128, D)
    wk = np.ascontiguousarray(
        L0_attn_wk.reshape(NKV, 128, IC, 128).transpose(0, 3, 2, 1)
    ).reshape(NKV, 128, D)
    wv = np.ascontiguousarray(
        L0_attn_wv.reshape(2, 512, IC, 128).transpose(0, 2, 3, 1))
    wo = np.ascontiguousarray(
        L0_attn_wo.reshape(OSLn, 512, IC, 128).transpose(0, 2, 3, 1))
    wf = np.ascontiguousarray(
        L0_ffn_weight.reshape(OSLn, 512, IC, 128).transpose(0, 2, 3, 1))
    sp = np.ascontiguousarray(L0_ffn_scale_param.reshape(1, D)).astype(np.float32)

    inv_freq = (1.0 / (500000.0 ** (np.arange(0, HD, 2, dtype=np.float32) / HD))
                ).astype(np.float32)
    t = np.arange(S, dtype=np.float32)
    freqs = np.outer(t, inv_freq).astype(np.float32)
    emb = np.concatenate([freqs, freqs], -1)
    cosT = np.ascontiguousarray(np.cos(emb).astype(np.float32).T)
    sinT = np.sin(emb).astype(np.float32).T
    sin2 = sinT.copy()
    sin2[0:64, :] *= -1.0
    sin2 = np.ascontiguousarray(sin2)
    tri = np.where(np.arange(128)[:, None] <= np.arange(128)[None, :],
                   0.0, -1.0e30).astype(np.float32)
    tri2 = np.ascontiguousarray(tri.T)
    eye = np.eye(128, dtype=np.float32)

    shared = {
        "wq_s": wq, "wk_s": wk, "wv_s": wv, "wo_s": wo, "wf_s": wf, "sp": sp,
        "ropecos": cosT, "ropesin2": sin2, "trimask": tri, "trimask2": tri2,
        "eye128": eye,
    }
    per_core = []
    for b in range(8):
        xb = np.ascontiguousarray(L0_attn_x[b])
        xt = np.ascontiguousarray(xb.T).reshape(IC, 128, S)
        m = dict(shared)
        m["xt"] = xt
        m["xn"] = xb
        per_core.append(m)
    return per_core


def kernel(L0_attn_x, L0_attn_wq, L0_attn_wk, L0_attn_wv, L0_attn_wo,
           L0_ffn_weight, L0_ffn_scale_param):
    global _NC_CACHE
    in_maps = _host_prep(np.asarray(L0_attn_x, np.float32),
                         np.asarray(L0_attn_wq, np.float32),
                         np.asarray(L0_attn_wk, np.float32),
                         np.asarray(L0_attn_wv, np.float32),
                         np.asarray(L0_attn_wo, np.float32),
                         np.asarray(L0_ffn_weight, np.float32),
                         np.asarray(L0_ffn_scale_param, np.float32))
    if _NC_CACHE is None:
        _NC_CACHE = build_nc()
    nc = _NC_CACHE
    res = run_bass_kernel_spmd(nc, in_maps, core_ids=list(range(8)))
    out = np.stack([np.asarray(r["out"], np.float32) for r in res.results], axis=0)
    return out


# version 24
# speedup vs baseline: 1.1565x; 1.1565x over previous
import math
from contextlib import ExitStack

import numpy as np
import ml_dtypes

import concourse.bass as bass
import concourse.mybir as mybir
from concourse.bass_utils import run_bass_kernel_spmd

F32 = mybir.dt.float32
BF16 = mybir.dt.bfloat16
AF = mybir.ActivationFunctionType
ALU = mybir.AluOpType
AX = mybir.AxisListType

S, D = 2048, 4096
NH, NKV, HD = 32, 8, 128
GQ = NH // NKV
IC = D // 128
SBn = S // 128
OSLn = D // 512
ISCALE = 1.0 / math.sqrt(HD)

ENGS = ("tensor", "vector", "scalar", "sync", "gpsimd")


class PB:

    def __init__(self, sems):
        self.sems = sems
        self.ctr = {k: 0 for k in sems}
        self.ev = {}
        self.ops = {e: [] for e in ENGS}
        self.pending = {e: [] for e in ENGS}

    def emit(self, eng, fn, waits=(), inc=None, inc_k=1, key=None):
        w = []
        for x in waits:
            if x is None:
                continue
            if isinstance(x, list):
                w += [y for y in x if y is not None]
            else:
                w.append(x)
        if self.pending[eng]:
            w = self.pending[eng] + w
            self.pending[eng] = []
        best = {}
        for (sname, val) in w:
            if best.get(sname, -1) < val:
                best[sname] = val
        w = list(best.items())
        if inc is not None:
            self.ctr[inc] += inc_k
            if key is not None:
                self.ev[key] = (inc, self.ctr[inc])
            self.ops[eng].append((fn, tuple(w), (inc, inc_k)))
        else:
            self.ops[eng].append((fn, tuple(w), None))

    def barrier(self):
        bw = [(k, v) for k, v in self.ctr.items() if v > 0]
        for e in ENGS:
            self.pending[e] = self.pending[e] + bw

    def flush(self, block):
        sems = self.sems

        def runner(engname):
            oplist = self.ops[engname]

            def run(eng):
                for fn, waits, inc in oplist:
                    for (sname, val) in waits:
                        eng.wait_ge(sems[sname], val)
                    inst = fn(eng)
                    if inc is not None:
                        inst.then_inc(sems[inc[0]], inc[1])
            return run

        block.tensor(runner("tensor"))
        block.vector(runner("vector"))
        block.scalar(runner("scalar"))
        block.sync(runner("sync"))
        block.gpsimd(runner("gpsimd"))


def build_nc(debug_taps=False):
    nc = bass.Bass("TRN2", target_bir_lowering=False, debug=False)
    dt = nc.dram_tensor
    i_xt = dt("xt", [IC, 128, S], F32, kind="ExternalInput").ap()
    i_xn = dt("xn", [S, D], F32, kind="ExternalInput").ap()
    i_wq = dt("wq_s", [NH, 128, D], F32, kind="ExternalInput").ap()
    i_wk = dt("wk_s", [NKV, 128, D], F32, kind="ExternalInput").ap()
    i_wv = dt("wv_s", [2, IC, 128, 512], F32, kind="ExternalInput").ap()
    i_wo = dt("wo_s", [OSLn, IC, 128, 512], F32, kind="ExternalInput").ap()
    i_wf = dt("wf_s", [OSLn, IC, 128, 512], F32, kind="ExternalInput").ap()
    i_sp = dt("sp", [1, D], F32, kind="ExternalInput").ap()
    i_cos = dt("ropecos", [128, S], F32, kind="ExternalInput").ap()
    i_sin2 = dt("ropesin2", [128, S], F32, kind="ExternalInput").ap()
    i_tri = dt("trimask", [128, 128], F32, kind="ExternalInput").ap()
    i_tri2 = dt("trimask2", [128, 128], F32, kind="ExternalInput").ap()
    i_eye = dt("eye128", [128, 128], F32, kind="ExternalInput").ap()
    o_out = dt("out", [S, D], F32, kind="ExternalOutput").ap()
    d_kt = dt("kt_scr", [NKV, 128, S], BF16, kind="Internal").ap()
    d_qt = dt("qt_scr", [NH, 128, S], BF16, kind="Internal").ap()
    d_v = dt("v_scr", [SBn, 128, 1024], BF16, kind="Internal").ap()
    d_at = dt("attn_scr", [NH, 128, S], BF16, kind="Internal").ap()
    d_xa = dt("xa_scr", [SBn, 128, D], F32, kind="Internal").ap()
    d_sg = dt("sgn_scr", [OSLn, 128, IC * 512], BF16, kind="Internal").ap()
    d_yf = dt("yf_scr", [SBn, 128, D], F32, kind="Internal").ap()
    d_cs = dt("cs_scr", [1, D], F32, kind="Internal").ap()

    glob = ExitStack()
    en = glob.enter_context
    semnames = (["dinig", "dinis", "dspl", "dwv", "dkv", "dah", "dxab", "dxab1",
                 "dcst", "dcsl", "m", "ve", "se"]
                + [f"dxt{j}" for j in range(4)]
                + [f"dsgc{j}" for j in range(4)]
                + [f"dws{j}" for j in range(2)] + [f"dsp{j}" for j in range(3)]
                + [f"dvs{j}" for j in range(2)] + [f"dqt{j}" for j in range(2)]
                + [f"dat{j}" for j in range(2)] + [f"dwo{j}" for j in range(2)]
                + [f"dxn{j}" for j in range(4)] + [f"dxa{j}" for j in range(2)]
                + [f"dsg{j}" for j in range(2)] + [f"dwc{j}" for j in range(4)]
                + [f"dxc{j}" for j in range(2)] + [f"dyf{j}" for j in range(2)]
                + [f"dyl{j}" for j in range(2)] + [f"dout{j}" for j in range(2)])
    sems = {n: en(nc.semaphore(n)) for n in semnames}
    pb = PB(sems)

    TRI = en(nc.sbuf_tensor("tri", [128, 128], F32))
    EYEB = en(nc.sbuf_tensor("eyeb", [128, 128], BF16))
    ONESB = en(nc.sbuf_tensor("onesb", [128, 1], BF16))
    ONESR = en(nc.sbuf_tensor("onesr", [1, 128], F32))
    SPSL = en(nc.sbuf_tensor("spsl", [1, 512], F32))
    SUM1 = en(nc.sbuf_tensor("sum1", [128, 128], F32))
    SQ1 = en(nc.sbuf_tensor("sq1", [128, 128], F32))
    M1 = en(nc.sbuf_tensor("m1", [128, SBn], F32))
    R1 = en(nc.sbuf_tensor("r1", [128, SBn], F32))
    SUM2 = en(nc.sbuf_tensor("sum2", [128, 128], F32))
    SQ2 = en(nc.sbuf_tensor("sq2", [128, 128], F32))
    M2 = en(nc.sbuf_tensor("m2", [128, SBn], F32))
    R2 = en(nc.sbuf_tensor("r2", [128, SBn], F32))
    TCOL = en(nc.sbuf_tensor("tcol", [128, 8], F32))
    STCOL = en(nc.sbuf_tensor("stcol", [128, 64], F32))
    DTCOL = en(nc.sbuf_tensor("dtcol", [128, 64], F32))
    CSROW = en(nc.sbuf_tensor("csrow", [1, 512], F32))
    ONE128 = en(nc.sbuf_tensor("one128", [128, 128], BF16))
    TRIMB = en(nc.sbuf_tensor("trimb", [128, 128], BF16))
    PS_S0 = en(nc.psum_tensor("ps_s0", [128, 1024], F32))
    PS_S1 = en(nc.psum_tensor("ps_s1", [128, 1024], F32))
    PS_A = en(nc.psum_tensor("ps_a", [128, 512], F32))
    PS_B = en(nc.psum_tensor("ps_b", [128, 512], F32))
    PS_C = en(nc.psum_tensor("ps_c", [128, 512], F32))
    PS_D = en(nc.psum_tensor("ps_d", [128, 512], F32))
    PS4 = [PS_A, PS_B, PS_C, PS_D]
    PSAB = [PS_A, PS_B]

    E = pb.emit
    W = lambda k: pb.ev[k]

    E("gpsimd", lambda g, o=EYEB[:], i=i_eye: g.dma_start(out=o, in_=i),
      inc="dinig", inc_k=16)
    E("gpsimd", lambda g, o=TRIMB[:], i=i_tri2: g.dma_start(out=o, in_=i),
      inc="dinig", inc_k=16)
    E("sync", lambda s, o=TRI[:], i=i_tri: s.dma_start(out=o, in_=i),
      inc="dinis", inc_k=16)
    E("vector", lambda v, o=ONESB[:]: v.memset(o, 1.0), inc="ve", key=("onesb",))
    E("vector", lambda v, o=ONESR[:]: v.memset(o, 1.0), inc="ve", key=("onesr",))
    E("vector", lambda v, o=ONE128[:]: v.memset(o, 1.0), inc="ve",
      key=("ones128",))

    sa = ExitStack()
    ea = sa.enter_context
    XT = ea(nc.sbuf_tensor("xtb", [128, IC * S], BF16))
    WS = [ea(nc.sbuf_tensor(f"ws{j}", [128, D], BF16)) for j in range(2)]
    WV = ea(nc.sbuf_tensor("wvb", [128, IC * 256], BF16))
    COS = ea(nc.sbuf_tensor("cosb", [128, S], F32))
    SIN2 = ea(nc.sbuf_tensor("sin2b", [128, S], F32))
    TMP1 = ea(nc.sbuf_tensor("tmp1", [128, 512], F32))
    TMP2 = ea(nc.sbuf_tensor("tmp2", [128, 512], F32))
    PESP = [ea(nc.sbuf_tensor(f"pesp{j}", [128, 512], BF16)) for j in range(3)]
    VSS = [ea(nc.sbuf_tensor(f"vss{j}", [128, 256], BF16)) for j in range(2)]
    WCH = [ea(nc.sbuf_tensor(f"wch{j}", [128, 512], BF16)) for j in range(4)]
    WABS = [ea(nc.sbuf_tensor(f"wabs{j}", [128, 512], BF16)) for j in range(4)]
    SGNC = [ea(nc.sbuf_tensor(f"sgnc{j}", [128, 512], BF16)) for j in range(4)]

    E("gpsimd", lambda g, o=WS[0][:], i=i_wk[0]: g.dma_start(out=o, in_=i),
      inc="dws0", inc_k=16, key=("k", "w", 0))
    for c in range(IC):
        xw = [W(("xt", c - 4))] if c >= 4 else []
        E("gpsimd", lambda g, o=XT[:, c * S:(c + 1) * S], i=i_xt[c]:
          g.dma_start(out=o, in_=i), waits=xw,
          inc=f"dxt{c % 4}", inc_k=16, key=("xt", c))
    E("sync", lambda s, o=COS[:], i=i_cos: s.dma_start(out=o, in_=i),
      inc="dinis", inc_k=16)
    E("sync", lambda s, o=SIN2[:], i=i_sin2: s.dma_start(out=o, in_=i),
      inc="dinis", inc_k=16)
    pb.ev[("eye",)] = ("dinig", 32)
    pb.ev[("trimb",)] = ("dinig", 32)
    for _k in [("tri",), ("cos",), ("sin",)]:
        pb.ev[_k] = ("dinis", 48)
    xt_all = [("dxt0", 16 * 8), ("dxt1", 16 * 8), ("dxt2", 16 * 8),
              ("dxt3", 16 * 8)]
    rope_rdy = [("dinis", 48)]

    ps_last = [[] for _ in range(4)]
    ws_last = [[] for _ in range(2)]
    pesp_last = [[] for _ in range(3)]
    vss_last = [[] for _ in range(2)]
    st_ctr = {"ps": 0, "pesp": 0, "vss": 0}

    rope_prev = []

    def rope_evac(psum, st, dst_ap, waits, key):
        c0 = st * 512
        E("vector", lambda v, o=TMP1[0:64, :], a=psum[64:128, :],
          b=SIN2[0:64, c0:c0 + 512]: v.tensor_tensor(o, a, b, op=ALU.mult),
          waits=list(waits) + rope_rdy + rope_prev, inc="ve")
        E("vector", lambda v, o=TMP1[64:128, :], a=psum[0:64, :],
          b=SIN2[64:128, c0:c0 + 512]: v.tensor_tensor(o, a, b, op=ALU.mult),
          inc="ve")
        E("vector", lambda v, o=TMP2[:], a=psum[:], b=COS[:, c0:c0 + 512]:
          v.tensor_tensor(o, a, b, op=ALU.mult), inc="ve", key=key + ("t2",))
        E("vector", lambda v, o=dst_ap, a=TMP1[:], b=TMP2[:]:
          v.tensor_add(o, a, b), waits=[W(key + ("t2",))], inc="ve", key=key)
        rope_prev[:] = [W(key)]

    wch_last = [[] for _ in range(4)]
    wabs_last = [[] for _ in range(4)]
    sgnc_last = [[] for _ in range(4)]
    psc_last = []
    sign_dma = []
    sign_cmp = []

    def mk_sign_dma(osl, c):
        def run():
            wsl = (osl * IC + c) % 4
            E("gpsimd", lambda g, o=WCH[wsl][:], i=i_wf[osl][c]:
              g.dma_start(out=o, in_=i),
              waits=[wch_last[wsl]], inc=f"dwc{wsl}", inc_k=16,
              key=("c", "wch", osl, c))
        return run

    def mk_sign_cmp(osl, c):
        def run():
            wsl = (osl * IC + c) % 4
            asl = (osl * IC + c) % 4
            E("scalar", lambda s, o=WABS[asl][:], i_=WCH[wsl][:]:
              s.activation(o, i_, AF.Abs),
              waits=[W(("c", "wch", osl, c))] + wabs_last[asl],
              inc="se", key=("c", "abs", osl, c))
            E("scalar", lambda s, o=SGNC[asl][:], i_=WCH[wsl][:]:
              s.activation(o, i_, AF.Sign),
              waits=sgnc_last[asl], inc="se", key=("c", "sgn", osl, c))
            wch_last[wsl] = [W(("c", "sgn", osl, c))]
            E("tensor", lambda t, o=PS_S1[:, 0:512], l=ONE128[:],
              rr=WABS[asl][:], s0=(c == 0), s1=(c == IC - 1):
              t.matmul(o, l, rr, start=s0, stop=s1),
              waits=[W(("c", "abs", osl, c)), W(("ones128",))]
              + (psc_last if c == 0 else []),
              inc="m", key=("c", "rsc", osl, c))
            wabs_last[asl] = [W(("c", "rsc", osl, c))]
            E("gpsimd", lambda g, o=d_sg[osl][:, c * 512:(c + 1) * 512],
              i=SGNC[asl][:]: g.dma_start(out=o, in_=i),
              waits=[W(("c", "sgn", osl, c))],
              inc=f"dsgc{asl}", inc_k=16, key=("c", "sgst", osl, c))
            sgnc_last[asl] = [W(("c", "sgst", osl, c))]
            if c == IC - 1:
                spw = [W(("c", "csr", osl - 1))] if osl >= 1 else []
                E("sync", lambda s, o=SPSL[:],
                  i=i_sp[:, osl * 512:(osl + 1) * 512]:
                  s.dma_start(out=o, in_=i), waits=spw,
                  inc="dspl", inc_k=16, key=("c", "spl", osl))
                cw = [W(("c", "rsc", osl, c))]
                if osl >= 1:
                    cw.append(W(("c", "csst", osl - 1)))
                E("vector", lambda v, o=CSROW[:], i0=PS_S1[0:1, 0:512]:
                  v.tensor_scalar(o, i0, 1.0 / D, 1e-8, op0=ALU.mult,
                                  op1=ALU.max),
                  waits=cw, inc="ve",
                  key=("c", "cs0", osl))
                psc_last[:] = [W(("c", "cs0", osl))]
                E("vector", lambda v, o=CSROW[:],
                  a=CSROW[:], b_=SPSL[:]:
                  v.tensor_tensor(o, a, b_, op=ALU.mult),
                  waits=[W(("c", "cs0", osl)), W(("c", "spl", osl))], inc="ve",
                  key=("c", "csr", osl))
                E("sync", lambda s, o=d_cs[:, osl * 512:(osl + 1) * 512],
                  i=CSROW[:]: s.dma_start(out=o, in_=i),
                  waits=[W(("c", "csr", osl))],
                  inc="dcst", inc_k=16, key=("c", "csst", osl))
        return run

    for osl in range(OSLn):
        for c in range(IC):
            sign_dma.append(mk_sign_dma(osl, c))
            sign_cmp.append(mk_sign_cmp(osl, c))
    sign_st = {"d": 0, "c": 0}

    def pop_sign(n_dma, n_cmp):
        for _ in range(n_cmp):
            if sign_st["c"] < sign_st["d"] - 2 and sign_st["c"] < 256:
                sign_cmp[sign_st["c"]]()
                sign_st["c"] += 1
        for _ in range(n_dma):
            if sign_st["d"] < 256:
                sign_dma[sign_st["d"]]()
                sign_st["d"] += 1

    def proj_T(nheads, wsrc, dst_scr, tag):
        for ob in range(nheads):
            sl = ob % 2
            strip = WS[sl]
            if not (tag == "k" and ob == 0):
                E("gpsimd", lambda g, o=strip[:], i=wsrc[ob]: g.dma_start(out=o, in_=i),
                  waits=[ws_last[sl]], inc=f"dws{sl}", inc_k=16, key=(tag, "w", ob))
            last_mm_key = None
            for st in range(4):
                if tag == "q":
                    pop_sign(2, 2)
                psl = st_ctr["ps"] % 4
                st_ctr["ps"] += 1
                psum = PS4[psl]
                chunked = (tag == "k" and ob < 2)
                waits = [W((tag, "w", ob)), ps_last[psl]]
                if not chunked:
                    waits.append(xt_all)
                for c in range(IC):
                    cw = waits if c == 0 else []
                    if chunked:
                        cw = cw + [W(("xt", c))]
                    E("tensor",
                      lambda t, o=psum[:], l=strip[:, c * 128:(c + 1) * 128],
                      r=XT[:, c * S + st * 512: c * S + st * 512 + 512],
                      s0=(c == 0), s1=(c == IC - 1):
                      t.matmul(o, l, r, start=s0, stop=s1),
                      waits=cw,
                      inc=("m" if c == IC - 1 else None),
                      key=((tag, "mm", ob, st) if c == IC - 1 else None))
                last_mm_key = (tag, "mm", ob, st)
                spl = st_ctr["pesp"] % 3
                st_ctr["pesp"] += 1
                sp = PESP[spl]
                rope_evac(psum, st, sp[:], [W(last_mm_key)] + pesp_last[spl],
                          (tag, "rope", ob, st))
                ps_last[psl] = [W((tag, "rope", ob, st))]
                E("sync", lambda s, o=dst_scr[ob][:, st * 512:(st + 1) * 512],
                  i=sp[:]: s.dma_start(out=o, in_=i),
                  waits=[W((tag, "rope", ob, st))],
                  inc=f"dsp{spl}", inc_k=16, key=(tag, "spill", ob, st))
                pesp_last[spl] = [W((tag, "spill", ob, st))]
            ws_last[sl] = [W(last_mm_key)]

    proj_T(NKV, i_wk, d_kt, "k")

    wv_last = []
    for og in range(4):
        h, off = og // 2, (og % 2) * 256
        E("gpsimd", lambda g, o=WV[:].rearrange("p (c o) -> p c o", c=IC),
          i=i_wv[h][:, :, off:off + 256].rearrange("c p o -> p c o"):
          g.dma_start(out=o, in_=i),
          waits=[wv_last], inc="dwv", inc_k=16, key=("v", "w", og))
        last_key = None
        for sb in range(SBn):
            psl = st_ctr["ps"] % 4
            st_ctr["ps"] += 1
            psum = PS4[psl]
            waits = [W(("v", "w", og)), ps_last[psl], xt_all]
            for c in range(IC):
                E("tensor",
                  lambda t, o=psum[:, 0:256],
                  l=XT[:, c * S + sb * 128: c * S + sb * 128 + 128],
                  r=WV[:, c * 256:(c + 1) * 256],
                  s0=(c == 0), s1=(c == IC - 1):
                  t.matmul(o, l, r, start=s0, stop=s1),
                  waits=(waits if c == 0 else ()),
                  inc=("m" if c == IC - 1 else None),
                  key=(("v", "mm", og, sb) if c == IC - 1 else None))
            last_key = ("v", "mm", og, sb)
            vsl = st_ctr["vss"] % 2
            st_ctr["vss"] += 1
            vs = VSS[vsl]
            E("scalar", lambda s, o=vs[:], i=psum[:, 0:256]: s.copy(o, i),
              waits=[W(("v", "mm", og, sb))] + vss_last[vsl],
              inc="se", key=("v", "ev", og, sb))
            ps_last[psl] = [W(("v", "ev", og, sb))]
            E("sync", lambda s, o=d_v[sb][:, og * 256:og * 256 + 256], i=vs[:]:
              s.dma_start(out=o, in_=i),
              waits=[W(("v", "ev", og, sb))], inc=f"dvs{vsl}", inc_k=16,
              key=("v", "spill", og, sb))
            vss_last[vsl] = [W(("v", "spill", og, sb))]
        wv_last = [W(last_key)]

    proj_T(NH, i_wq, d_qt, "q")
    while sign_st["c"] < 256:
        sign_cmp[sign_st["c"]]()
        sign_st["c"] += 1

    pb.barrier()
    sa.close()

    sb_ = ExitStack()
    eb = sb_.enter_context
    KT = eb(nc.sbuf_tensor("ktb", [128, NKV * S], BF16))
    VSB = eb(nc.sbuf_tensor("vsbt", [128, SBn * 1024], BF16))
    QT = [eb(nc.sbuf_tensor(f"qtb{j}", [128, S], BF16)) for j in range(2)]
    PRBP = [eb(nc.sbuf_tensor(f"prbp{j}", [128, 1024], BF16)) for j in range(3)]
    LNT = [eb(nc.sbuf_tensor(f"lnt{j}", [128, 512], F32)) for j in range(2)]
    RDB = [eb(nc.sbuf_tensor(f"rdb{j}", [128, 512], F32)) for j in range(2)]
    ATTN = [eb(nc.sbuf_tensor(f"attn{j}", [128, S], BF16)) for j in range(2)]

    E("sync", lambda s, o=KT[:].rearrange("p (o s) -> p o s", o=NKV),
      i=d_kt.rearrange("o p s -> p o s"):
      s.dma_start(out=o, in_=i), inc="dkv", inc_k=16, key=("B", "kt"))
    E("sync", lambda s, o=VSB[:].rearrange("p (j o) -> p j o", j=SBn),
      i=d_v.rearrange("j p o -> p j o"):
      s.dma_start(out=o, in_=i), inc="dkv", inc_k=16, key=("B", "vl"))
    pb.ev[("B", "kt")] = ("dkv", 32)
    pb.ev[("B", "vl")] = ("dkv", 32)
    E("sync", lambda s, o=QT[0][:], i=d_qt[0]: s.dma_start(out=o, in_=i),
      inc="dqt0", inc_k=16, key=("B", "qt", 0))

    PSP = [PS_S0, PS_S1]

    def den_ap(dsl, c0=0):
        return (PS_C if dsl == 0 else PS_D)[:, c0:512]

    def pv_ap(pvsl, c0=0):
        return (PS_A if pvsl == 0 else PS_B)[:, c0:512]

    sc_last = [[], []]
    prb_last = [[], [], []]
    pv_last = [[], []]
    den_rdy = [[], []]
    lnt_war = [[], []]
    rdb_war = [[], []]
    attn_last = [[], []]
    bst = {"pr": 0}
    pend_dp = []
    pend_epi = []

    def emit_epi(h, qc, dsl, pvsl):
        def run():
            E("scalar", lambda s, o=LNT[dsl][:], i_=den_ap(dsl):
              s.activation(o, i_, AF.Ln),
              waits=[W(("B", "dn", h, qc))] + lnt_war[dsl],
              inc="se", key=("B", "lnd", h, qc))
            den_rdy[dsl] = [W(("B", "lnd", h, qc))]
            E("scalar", lambda s, o=RDB[dsl][:], i_=LNT[dsl][:]:
              s.activation(o, i_, AF.Exp, scale=-1.0),
              waits=[W(("B", "lnd", h, qc))] + rdb_war[dsl],
              inc="se", key=("B", "rcd", h, qc))
            lnt_war[dsl] = [W(("B", "rcd", h, qc))]
            w = [W(("B", "pv", h, qc)), W(("B", "rcd", h, qc))]
            if qc == 0:
                w += attn_last[h % 2]
            E("vector", lambda v, o=ATTN[h % 2][:, qc * 512:(qc + 1) * 512],
              a=pv_ap(pvsl), b_=RDB[dsl][:]:
              v.tensor_tensor(o, a, b_, op=ALU.mult),
              waits=w, inc="ve", key=("B", "pve", h, qc))
            pv_last[pvsl] = [W(("B", "pve", h, qc))]
            rdb_war[dsl] = [W(("B", "pve", h, qc))]
            if qc == 3:
                E("sync", lambda s, o=d_at[h], i=ATTN[h % 2][:]:
                  s.dma_start(out=o, in_=i),
                  waits=[W(("B", "pve", h, 3))],
                  inc=f"dat{h % 2}", inc_k=16, key=("B", "aspill", h))
                attn_last[h % 2] = [W(("B", "aspill", h))]
        return run

    def emit_dp(h, qc, j, gi, dsl, pvsl, prsl, njs, kv):
        def run():
            c0 = max(0, j - 4 * qc) * 128
            po = (j % 2) * 512
            dw = [W(("B", "exp", h, qc, j // 2)), W(("ones128",))]
            if j == 0:
                dw += den_rdy[dsl]
            E("tensor", lambda t, o=den_ap(dsl, c0), l=ONE128[:],
              rr=PRBP[prsl][:, po + c0: po + 512],
              s0=(j == 0), s1=(j == njs - 1):
              t.matmul(o, l, rr, start=s0, stop=s1),
              waits=dw, inc="m",
              key=(("B", "dn", h, qc) if j == njs - 1 else ("B", "dnj", gi, j)))
            pw = [W(("B", "exp", h, qc, j // 2)), W(("B", "vl"))]
            if j == 0:
                pw += pv_last[pvsl]
            E("tensor", lambda t, o=pv_ap(pvsl, c0),
              l=VSB[:, j * 1024 + kv * 128: j * 1024 + (kv + 1) * 128],
              rr=PRBP[prsl][:, po + c0: po + 512],
              s0=(j == 0), s1=(j == njs - 1):
              t.matmul(o, l, rr, start=s0, stop=s1),
              waits=pw, inc="m",
              key=(("B", "pv", h, qc) if j == njs - 1 else ("B", "pvj", gi, j)))
            if j % 2 == 1:
                prb_last[prsl] = [W(("B", "pv", h, qc) if j == njs - 1
                                    else ("B", "pvj", gi, j))]
        return run

    for h in range(NH):
        kv = h // GQ
        if h + 1 < NH:
            qw = [W(("B", "sc", h - 1, 3, 15))] if h >= 1 else []
            E("sync", lambda s, o=QT[(h + 1) % 2][:], i=d_qt[h + 1]:
              s.dma_start(out=o, in_=i), waits=qw,
              inc=f"dqt{(h + 1) % 2}", inc_k=16, key=("B", "qt", h + 1))
        for qc in range(4):
            gi = h * 4 + qc
            njs = 4 * qc + 4
            dsl = gi % 2
            pvsl = gi % 2
            for j in range(njs):
                c0 = max(0, j - 4 * qc) * 128
                pr = bst["pr"]
                ssl = pr % 2
                prsl = pr % 3
                po = (j % 2) * 512
                scw = [W(("B", "qt", h)), W(("B", "kt"))]
                if j % 2 == 0:
                    scw = scw + sc_last[ssl]
                E("tensor",
                  lambda t, o=PSP[ssl][:, po + c0: po + 512],
                  l=KT[:, kv * S + j * 128: kv * S + (j + 1) * 128],
                  rr=QT[h % 2][:, qc * 512 + c0: (qc + 1) * 512]:
                  t.matmul(o, l, rr, start=True, stop=True),
                  waits=scw, inc="m", key=("B", "sc", h, qc, j))
                if j >= 4 * qc:
                    E("tensor",
                      lambda t, o=PSP[ssl][:, po + c0: po + c0 + 128],
                      l=TRIMB[:], rr=EYEB[:]:
                      t.matmul(o, l, rr, start=False, stop=True),
                      waits=[W(("trimb",)), W(("eye",))],
                      inc="m", key=("B", "mk", h, qc, j))
                if j % 2 == 1:
                    c0e = max(0, (j - 1) - 4 * qc) * 128
                    if j >= 4 * qc:
                        expw = [W(("B", "mk", h, qc, j))]
                    else:
                        expw = [W(("B", "sc", h, qc, j))]
                    E("scalar", lambda s, o=PRBP[prsl][:, c0e:1024],
                      i_=PSP[ssl][:, c0e:1024]:
                      s.activation(o, i_, AF.Exp, scale=ISCALE),
                      waits=expw + prb_last[prsl], inc="se",
                      key=("B", "exp", h, qc, j // 2))
                    sc_last[ssl] = [W(("B", "exp", h, qc, j // 2))]
                    bst["pr"] += 1
                    pend_dp.append(emit_dp(h, qc, j - 1, gi, dsl, pvsl, prsl,
                                           njs, kv))
                    pend_dp.append(emit_dp(h, qc, j, gi, dsl, pvsl, prsl,
                                           njs, kv))
                    if len(pend_dp) > 4:
                        pend_dp.pop(0)()
                        pend_dp.pop(0)()
                if j == 3 and pend_epi:
                    pend_epi.pop(0)()
            pend_epi.append(emit_epi(h, qc, dsl, pvsl))
    while pend_dp:
        pend_dp.pop(0)()
    while pend_epi:
        pend_epi.pop(0)()

    pb.barrier()
    sb_.close()

    sw = ExitStack()
    ew_ = sw.enter_context
    ATTNH = ew_(nc.sbuf_tensor("attnh", [128, IC * 1024], BF16))
    WOS = [ew_(nc.sbuf_tensor(f"wos{j}", [128, IC * 512], BF16)) for j in range(2)]
    XNT = [ew_(nc.sbuf_tensor(f"xnt{j}", [128, 512], F32)) for j in range(4)]
    XAT = [ew_(nc.sbuf_tensor(f"xat{j}", [128, 512], F32)) for j in range(4)]
    SQT = ew_(nc.sbuf_tensor("sqtb", [128, 512], BF16))

    def psw(slot):
        if slot == 0:
            return PS_A[:]
        if slot == 1:
            return PS_B[:]
        if slot == 2:
            return PS_S0[:, 0:512]
        if slot == 3:
            return PS_S0[:, 512:1024]
        if slot == 4:
            return PS_C[:]
        if slot == 5:
            return PS_D[:]
        if slot == 6:
            return PS_S1[:, 0:512]
        return PS_S1[:, 512:1024]

    wos_last = [[], []]
    sqt_prev = []
    xaspill_prev = [[], []]
    xnt_last = [[] for _ in range(4)]
    xat_last = [[] for _ in range(4)]
    psw_last = [[] for _ in range(8)]
    attnh_last = []

    for half in range(2):
        for osl in range(2):
            E("gpsimd", lambda g, o=WOS[osl % 2][:].rearrange("p (c o) -> p c o", c=IC),
              i=i_wo[osl].rearrange("c p o -> p c o"): g.dma_start(out=o, in_=i),
              waits=[wos_last[osl % 2]], inc=f"dwo{osl % 2}", inc_k=16,
              key=("w1", "w", half, osl))
        E("sync", lambda s, o=ATTNH[:, 0:16 * 1024].rearrange("p (c s) -> p c s", c=16),
          i=d_at[0:16, :, half * 1024:(half + 1) * 1024].rearrange("c p s -> p c s"):
          s.dma_start(out=o, in_=i), waits=[attnh_last],
          inc="dah", inc_k=16, key=("w1", "ah_a", half))
        E("gpsimd", lambda g, o=ATTNH[:, 16 * 1024:32 * 1024].rearrange("p (c s) -> p c s", c=16),
          i=d_at[16:32, :, half * 1024:(half + 1) * 1024].rearrange("c p s -> p c s"):
          g.dma_start(out=o, in_=i), waits=[attnh_last],
          inc="dah", inc_k=16, key=("w1", "ah_b", half))
        pb.ev[("w1", "ah", half)] = ("dah", 32 * (half + 1))
        for osl in range(OSLn):
            if osl >= 2:
                E("gpsimd", lambda g, o=WOS[osl % 2][:].rearrange("p (c o) -> p c o", c=IC),
                  i=i_wo[osl].rearrange("c p o -> p c o"): g.dma_start(out=o, in_=i),
                  waits=[wos_last[osl % 2]], inc=f"dwo{osl % 2}", inc_k=16,
                  key=("w1", "w", half, osl))
            for sbl in range(8):
                sb = half * 8 + sbl
                idx = osl * 8 + sbl
                psl = idx % 8
                col = sb * 8 + osl
                xsl = idx % 4
                E("sync", lambda s, o=XNT[xsl][:],
                  i=i_xn[sb * 128:(sb + 1) * 128, osl * 512:(osl + 1) * 512]:
                  s.dma_start(out=o, in_=i), waits=[xnt_last[xsl]],
                  inc=f"dxn{xsl}", inc_k=16, key=("w1", "xn", half, idx))
                mmw = [W(("w1", "w", half, osl)), W(("w1", "ah", half))] + psw_last[psl]
                for c in range(IC):
                    E("tensor",
                      lambda t, o=psw(psl),
                      l=ATTNH[:, c * 1024 + sbl * 128: c * 1024 + (sbl + 1) * 128],
                      r=WOS[osl % 2][:, c * 512:(c + 1) * 512],
                      s0=(c == 0), s1=(c == IC - 1):
                      t.matmul(o, l, r, start=s0, stop=s1),
                      waits=(mmw if c == 0 else ()),
                      inc=("m" if c == IC - 1 else None),
                      key=(("w1", "mm", half, osl, sbl) if c == IC - 1 else None))
                xa = XAT[idx % 4]
                E("vector", lambda v, o=xa[:], a=psw(psl), b_=XNT[xsl][:]:
                  v.tensor_add(o, a, b_),
                  waits=[W(("w1", "mm", half, osl, sbl)), W(("w1", "xn", half, idx))]
                  + xat_last[idx % 4],
                  inc="ve", key=("w1", "xa", half, idx))
                psw_last[psl] = [W(("w1", "xa", half, idx))]
                xnt_last[xsl] = [W(("w1", "xa", half, idx))]
                E("vector", lambda v, o=SUM1[:, col:col + 1], i_=xa[:]:
                  v.reduce_sum(o, i_, axis=AX.X), inc="ve",
                  key=(("w1", "s1w", sb) if osl == OSLn - 1 else None))
                E("scalar", lambda s, o=SQT[:], i_=xa[:], ac=SQ1[:, col:col + 1]:
                  s.activation(o, i_, AF.Square, accum_out=ac),
                  waits=[W(("w1", "xa", half, idx))] + sqt_prev,
                  inc="se", key=("w1", "sq", sb, osl))
                sqt_prev[:] = [W(("w1", "sq", sb, osl))]
                E("sync", lambda s, o=d_xa[sb][:, osl * 512:(osl + 1) * 512], i=xa[:]:
                  s.dma_start(out=o, in_=i),
                  waits=[W(("w1", "xa", half, idx)), xaspill_prev[sbl % 2]],
                  inc=f"dxa{sbl % 2}", inc_k=16, key=("w1", "xast", sb, osl))
                xaspill_prev[sbl % 2] = [W(("w1", "xast", sb, osl))]
                xat_last[idx % 4] = [W(("w1", "xast", sb, osl)), W(("w1", "sq", sb, osl))]
            wos_last[osl % 2] = [W(("w1", "mm", half, osl, 7))]
        attnh_last = [W(("w1", "mm", half, OSLn - 1, 7))]

    pb.barrier()
    sw.close()

    sc_ = ExitStack()
    ec = sc_.enter_context
    Y1T = ec(nc.sbuf_tensor("y1t", [128, IC * 1024], BF16))
    XABH = [ec(nc.sbuf_tensor(f"xabh{j}", [128, 2048], F32)) for j in range(2)]
    SGN = [ec(nc.sbuf_tensor(f"sgn{j}", [128, IC * 512], BF16)) for j in range(2)]
    CSRA = [ec(nc.sbuf_tensor(f"csra{j}", [1, 512], F32)) for j in range(2)]
    CSBIG = ec(nc.sbuf_tensor("csbig", [128, D], F32))
    Y1C = [ec(nc.sbuf_tensor(f"y1c{j}", [128, 512], BF16)) for j in range(4)]
    DIAGR = [ec(nc.sbuf_tensor(f"diagr{j}", [128, 128], BF16)) for j in range(2)]
    XCT = [ec(nc.sbuf_tensor(f"xct{j}", [128, 512], F32)) for j in range(2)]
    TFB = ec(nc.sbuf_tensor("tfb", [128, 512], F32))
    TYB = ec(nc.sbuf_tensor("tyb", [128, 512], F32))
    TYF = [ec(nc.sbuf_tensor(f"tyf{j}", [128, 512], F32)) for j in range(2)]
    SQT2 = ec(nc.sbuf_tensor("sqt2", [128, 512], BF16))
    YFD = [ec(nc.sbuf_tensor(f"yfd{j}", [128, 1024], F32)) for j in range(2)]
    OUTD = [ec(nc.sbuf_tensor(f"outd{j}", [128, 1024], F32)) for j in range(2)]
    y1t_view = Y1T[:].rearrange("p (c q) -> p c q", c=IC)

    def psr(slot, lo=0, hi=512):
        if slot == 0:
            return PS_A[:, lo:hi]
        if slot == 1:
            return PS_B[:, lo:hi]
        if slot == 2:
            return PS_S0[:, lo:hi]
        if slot == 3:
            return PS_S0[:, 512 + lo:512 + hi]
        if slot == 4:
            return PS_C[:, lo:hi]
        if slot == 5:
            return PS_D[:, lo:hi]
        if slot == 6:
            return PS_S1[:, lo:hi]
        return PS_S1[:, 512 + lo:512 + hi]

    cst = {"ps": 0}
    xabh_last = [[], []]
    y1c_last = [[] for _ in range(4)]
    diagr_last = [[], []]
    psab_last = [[] for _ in range(8)]
    pscd_last = [[], []]
    tfb_prev = []
    sqt2_prev = []
    sgn_last = [[], []]
    xct_last = [[], []]
    tyf_last = [[], []]
    yfd_last = [[], []]
    outd_last = [[], []]
    y1t_war = []

    def emit_csbig():
        for osl in range(OSLn):
            csl = osl % 2
            lw = [W(("c", "csb", osl - 1))] if osl >= 1 else []
            E("sync", lambda s, o=CSRA[csl][:],
              i=d_cs[:, osl * 512:(osl + 1) * 512]: s.dma_start(out=o, in_=i),
              waits=lw, inc="dcsl", inc_k=16, key=("c", "csra", osl))
            E("tensor", lambda t, o=(PS_C if csl == 0 else PS_D)[:],
              l=ONESR[:], rr=CSRA[csl][:]:
              t.matmul(o, l, rr, start=True, stop=True),
              waits=[W(("onesr",)), W(("c", "csra", osl))] + pscd_last[csl],
              inc="m", key=("c", "csb", osl))
            E("vector", lambda v, o=CSBIG[:, osl * 512:(osl + 1) * 512],
              i_=(PS_C if csl == 0 else PS_D)[:]:
              v.tensor_copy(o, i_),
              waits=[W(("c", "csb", osl))],
              inc="ve", key=("c", "csbe", osl))
            pscd_last[csl] = [W(("c", "csbe", osl))]
        psab_last[4] = list(pscd_last[0])
        psab_last[5] = list(pscd_last[1])

    def emit_stats1():
        for sb in range(16):
            tc = lambda k, sb=sb: STCOL[:, sb * 4 + k: sb * 4 + k + 1]
            E("vector", lambda v, o=tc(0), i_=SUM1[:, sb * 8:sb * 8 + 8]:
              v.reduce_sum(o, i_, axis=AX.X),
              waits=[W(("w1", "sq", sb, 7))], inc="ve", key=("b2", "s0", sb))
            E("vector", lambda v, o=M1[:, sb:sb + 1], a=tc(0):
              v.tensor_scalar_mul(o, a, 1.0 / D),
              waits=[W(("b2", "s0", sb))], inc="ve", key=("b2", "s1", sb))
            E("vector", lambda v, o=tc(1), i_=SQ1[:, sb * 8:sb * 8 + 8]:
              v.reduce_sum(o, i_, axis=AX.X),
              inc="ve", key=("b2", "s2", sb))
            E("vector", lambda v, o=tc(1), a=tc(1):
              v.tensor_scalar(o, a, 1.0 / D, 1.0e-5, op0=ALU.mult, op1=ALU.add),
              waits=[W(("b2", "s2", sb))], inc="ve", key=("b2", "s3", sb))
            E("vector", lambda v, o=tc(2), a=M1[:, sb:sb + 1], b_=M1[:, sb:sb + 1]:
              v.tensor_tensor(o, a, b_, op=ALU.mult),
              waits=[W(("b2", "s1", sb))], inc="ve", key=("b2", "s4", sb))
            E("vector", lambda v, o=tc(1), a=tc(1), b_=tc(2):
              v.tensor_tensor(o, a, b_, op=ALU.subtract),
              waits=[W(("b2", "s3", sb)), W(("b2", "s4", sb))],
              inc="ve", key=("b2", "var", sb))
            E("scalar", lambda s, o=tc(3), i_=tc(1):
              s.activation(o, i_, AF.Ln),
              waits=[W(("b2", "var", sb))], inc="se", key=("b2", "ln", sb))
            E("scalar", lambda s, o=R1[:, sb:sb + 1], i_=tc(3):
              s.activation(o, i_, AF.Exp, scale=-0.5),
              waits=[W(("b2", "ln", sb))], inc="se", key=("b2", "r1", sb))

    def emit_b2(half):
        nonlocal y1t_war
        for sbl in range(8):
            sb = half * 8 + sbl
            for hi in range(2):
                par = (sb * 2 + hi) % 2
                E("gpsimd", lambda g, o=XABH[par][:],
                  i=d_xa[sb][:, hi * 2048:(hi + 1) * 2048]:
                  g.dma_start(out=o, in_=i),
                  waits=[xabh_last[par], W(("w1", "xast", sb, 7))],
                  inc=("dxab" if par == 0 else "dxab1"), inc_k=16,
                  key=("b2", "xab", sb, hi))
            dsl = sbl % 2
            E("vector", lambda v, o=DIAGR[dsl][:], a=EYEB[:], sc=R1[:, sb:sb + 1]:
              v.tensor_scalar_mul(o, a, sc),
              waits=[W(("b2", "r1", sb)), W(("eye",))] + diagr_last[dsl],
              inc="ve", key=("b2", "dgr", sb))
            for g in range(8):
                hi = g // 4
                par = (sb * 2 + hi) % 2
                lc = g * 512 - hi * 2048
                ysl = g % 4
                E("vector", lambda v, o=Y1C[ysl][:],
                  i0=XABH[par][:, lc:lc + 512], sc=M1[:, sb:sb + 1]:
                  v.tensor_single_scalar(o, i0, sc, op=ALU.subtract),
                  waits=[W(("b2", "xab", sb, hi)), W(("b2", "s1", sb))]
                  + y1c_last[ysl],
                  inc="ve", key=("b2", "y1c", sb, g))
                if g == 3 or g == 7:
                    xabh_last[par] = [W(("b2", "y1c", sb, g))]
                psl = cst["ps"] % 8
                cst["ps"] += 1
                for u in range(4):
                    E("tensor",
                      lambda t, o=psr(psl, u * 128, (u + 1) * 128),
                      l=Y1C[ysl][:, u * 128:(u + 1) * 128], rr=DIAGR[dsl][:]:
                      t.matmul(o, l, rr, start=True, stop=True),
                      waits=([W(("b2", "y1c", sb, g)), W(("b2", "dgr", sb))]
                             + psab_last[psl] if u == 0 else ()),
                      inc=("m" if u == 3 else None),
                      key=(("b2", "tp", sb, g) if u == 3 else None))
                E("scalar",
                  lambda s, o=y1t_view[:, g * 4:(g + 1) * 4, sbl * 128:(sbl + 1) * 128],
                  i_=psr(psl).rearrange("p (u l) -> p u l", u=4):
                  s.copy(o, i_),
                  waits=[W(("b2", "tp", sb, g))] + y1t_war,
                  inc="se", key=("b2", "y1t", sb, g))
                psab_last[psl] = [W(("b2", "y1t", sb, g))]
                y1c_last[ysl] = [W(("b2", "tp", sb, g))]
            diagr_last[dsl] = [W(("b2", "tp", sb, 7))]
        y1t_war = []

    def emit_c(half, gi, sbls, dpops=(), set_war=False):
        nonlocal y1t_war
        dpops = list(dpops)
        ndp = len(dpops)
        npop = 0
        E("gpsimd", lambda g, o=SGN[0][:], i=d_sg[0]: g.dma_start(out=o, in_=i),
          waits=[sgn_last[0]],
          inc="dsg0", inc_k=16, key=("c", "sgld", half, gi, 0))
        for osl in range(OSLn):
            ssl = osl % 2
            if osl + 1 < OSLn:
                nsl = (osl + 1) % 2
                E("gpsimd", lambda g, o=SGN[nsl][:], i=d_sg[osl + 1]:
                  g.dma_start(out=o, in_=i),
                  waits=[sgn_last[nsl]],
                  inc=f"dsg{nsl}", inc_k=16, key=("c", "sgld", half, gi, osl + 1))
            sgn_rdy = W(("c", "sgld", half, gi, osl))
            while npop < (osl + 1) * ndp // OSLn:
                dpops.pop(0)()
                npop += 1
            for sbl in sbls:
                sb = half * 8 + sbl
                idx = osl * 8 + sbl
                psl = cst["ps"] % 8
                cst["ps"] += 1
                col = sb * 8 + osl
                xsl = idx % 2
                E("sync", lambda s, o=XCT[xsl][:],
                  i=d_xa[sb][:, osl * 512:(osl + 1) * 512]:
                  s.dma_start(out=o, in_=i),
                  waits=[xct_last[xsl], W(("w1", "xast", sb, osl))],
                  inc=f"dxc{xsl}", inc_k=16, key=("c", "xct", half, idx))
                for c in range(IC):
                    E("tensor",
                      lambda t, o=psr(psl),
                      l=Y1T[:, c * 1024 + sbl * 128: c * 1024 + (sbl + 1) * 128],
                      r=SGN[ssl][:, c * 512:(c + 1) * 512],
                      s0=(c == 0), s1=(c == IC - 1):
                      t.matmul(o, l, r, start=s0, stop=s1),
                      waits=([sgn_rdy, W(("b2", "y1t", sb, 7))] + psab_last[psl]
                             if c == 0 else ()),
                      inc=("m" if c == IC - 1 else None),
                      key=(("c", "mm", half, osl, sbl) if c == IC - 1 else None))
                E("vector", lambda v, o=TFB[:], a=psr(psl),
                  b_=CSBIG[:, osl * 512:(osl + 1) * 512]:
                  v.tensor_tensor(o, a, b_, op=ALU.mult),
                  waits=[W(("c", "mm", half, osl, sbl)),
                         W(("c", "csbe", osl))] + tfb_prev,
                  inc="ve", key=("c", "tf", half, idx))
                psab_last[psl] = [W(("c", "tf", half, idx))]
                E("vector", lambda v, o=TFB[:], i0=TFB[:]:
                  v.tensor_scalar(o, i0, 1.0e6, -1.0e6, op0=ALU.min, op1=ALU.max),
                  waits=[W(("c", "tf", half, idx))],
                  inc="ve", key=("c", "cl", half, idx))
                E("vector", lambda v, o=TYB[:], i0=XCT[xsl][:], s1=M1[:, sb:sb + 1],
                  s2=R1[:, sb:sb + 1]:
                  v.tensor_scalar(o, i0, s1, s2, op0=ALU.subtract, op1=ALU.mult),
                  waits=[W(("c", "xct", half, idx)), W(("b2", "r1", sb))]
                  + tfb_prev,
                  inc="ve", key=("c", "tyb", half, idx))
                tf = TYF[idx % 2]
                E("vector", lambda v, o=tf[:], a=TFB[:], b_=TYB[:]:
                  v.tensor_add(o, a, b_),
                  waits=tyf_last[idx % 2] + [W(("c", "tyb", half, idx)),
                                             W(("c", "cl", half, idx))],
                  inc="ve", key=("c", "yf", half, idx))
                tfb_prev[:] = [W(("c", "yf", half, idx))]
                xct_last[xsl] = [W(("c", "yf", half, idx))]
                E("vector", lambda v, o=SUM2[:, col:col + 1], i_=tf[:]:
                  v.reduce_sum(o, i_, axis=AX.X), inc="ve",
                  key=(("c", "s2w", sb) if osl == OSLn - 1 else None))
                E("scalar", lambda s, o=SQT2[:], i_=tf[:], ac=SQ2[:, col:col + 1]:
                  s.activation(o, i_, AF.Square, accum_out=ac),
                  waits=[W(("c", "yf", half, idx))] + sqt2_prev,
                  inc="se", key=("c", "sq2", sb, osl))
                sqt2_prev[:] = [W(("c", "sq2", sb, osl))]
                E("sync", lambda s, o=d_yf[sb][:, osl * 512:(osl + 1) * 512], i=tf[:]:
                  s.dma_start(out=o, in_=i),
                  waits=[W(("c", "yf", half, idx))],
                  inc=f"dyf{idx % 2}", inc_k=16, key=("c", "yfst", sb, osl))
                tyf_last[idx % 2] = [W(("c", "yfst", sb, osl)), W(("c", "sq2", sb, osl))]
            sgn_last[ssl] = [W(("c", "mm", half, osl, sbls[-1]))]
        while dpops:
            dpops.pop(0)()
        if set_war:
            y1t_war = [W(("c", "mm", half, OSLn - 1, sbls[-1]))]

    def emit_d(half, sbset=tuple(range(8))):
        cls = []
        for sbl_ in sbset:
            def runsb(sbl=sbl_):
                sb = half * 8 + sbl
                tc = lambda k, sb=sb: DTCOL[:, sb * 4 + k: sb * 4 + k + 1]
                E("vector", lambda v, o=tc(0), i_=SUM2[:, sb * 8:sb * 8 + 8]:
                  v.reduce_sum(o, i_, axis=AX.X),
                  waits=[W(("c", "sq2", sb, 7)), W(("c", "s2w", sb)),
                         W(("c", "yf", half, 7 * 8 + sbl))],
                  inc="ve", key=("d", "s0", sb))
                E("vector", lambda v, o=M2[:, sb:sb + 1], a=tc(0):
                  v.tensor_scalar_mul(o, a, 1.0 / D),
                  waits=[W(("d", "s0", sb))], inc="ve", key=("d", "s1", sb))
                E("vector", lambda v, o=tc(1), i_=SQ2[:, sb * 8:sb * 8 + 8]:
                  v.reduce_sum(o, i_, axis=AX.X),
                  inc="ve", key=("d", "s2", sb))
                E("vector", lambda v, o=tc(1), a=tc(1):
                  v.tensor_scalar(o, a, 1.0 / D, 1.0e-5, op0=ALU.mult, op1=ALU.add),
                  waits=[W(("d", "s2", sb))], inc="ve", key=("d", "s3", sb))
                E("vector", lambda v, o=tc(2), a=M2[:, sb:sb + 1],
                  b_=M2[:, sb:sb + 1]:
                  v.tensor_tensor(o, a, b_, op=ALU.mult),
                  waits=[W(("d", "s1", sb))], inc="ve", key=("d", "s4", sb))
                E("vector", lambda v, o=tc(1), a=tc(1), b_=tc(2):
                  v.tensor_tensor(o, a, b_, op=ALU.subtract),
                  waits=[W(("d", "s3", sb)), W(("d", "s4", sb))],
                  inc="ve", key=("d", "var", sb))
                E("scalar", lambda s, o=tc(3), i_=tc(1):
                  s.activation(o, i_, AF.Ln),
                  waits=[W(("d", "var", sb))], inc="se", key=("d", "ln", sb))
                E("scalar", lambda s, o=R2[:, sb:sb + 1], i_=tc(3):
                  s.activation(o, i_, AF.Exp, scale=-0.5),
                  waits=[W(("d", "ln", sb))], inc="se", key=("d", "r2", sb))
                for ch in range(4):
                    idx = sbl * 4 + ch
                    ysl = idx % 2
                    E("gpsimd", lambda g, o=YFD[ysl][:],
                      i=d_yf[sb][:, ch * 1024:(ch + 1) * 1024]:
                      g.dma_start(out=o, in_=i),
                      waits=[yfd_last[ysl], W(("c", "yfst", sb, 2 * ch)),
                             W(("c", "yfst", sb, 2 * ch + 1))],
                      inc=f"dyl{ysl}", inc_k=16, key=("d", "yft", half, idx))
                    E("vector", lambda v, o=OUTD[ysl][:], i0=YFD[ysl][:],
                      s1=M2[:, sb:sb + 1], s2=R2[:, sb:sb + 1]:
                      v.tensor_scalar(o, i0, s1, s2, op0=ALU.subtract, op1=ALU.mult),
                      waits=[W(("d", "yft", half, idx)), W(("d", "r2", sb))]
                      + outd_last[ysl],
                      inc="ve", key=("d", "ot", half, idx))
                    yfd_last[ysl] = [W(("d", "ot", half, idx))]
                    E("sync", lambda s, o=o_out[sb * 128:(sb + 1) * 128,
                                              ch * 1024:(ch + 1) * 1024],
                      i=OUTD[ysl][:]: s.dma_start(out=o, in_=i),
                      waits=[W(("d", "ot", half, idx))],
                      inc=f"dout{ysl}", inc_k=16, key=("d", "ost", half, idx))
                    outd_last[ysl] = [W(("d", "ost", half, idx))]
            cls.append(runsb)
        return cls

    emit_csbig()
    emit_stats1()
    emit_b2(0)
    emit_c(0, 0, list(range(8)), set_war=True)
    emit_b2(1)
    emit_c(1, 0, list(range(4)), dpops=emit_d(0))
    emit_c(1, 1, list(range(4, 8)), dpops=emit_d(1, (0, 1, 2, 3)))
    for fn in emit_d(1, (4, 5, 6, 7)):
        fn()

    if debug_taps:
        taps = [("tap_kt", d_kt, [NKV, 128, S], BF16), ("tap_qt", d_qt, [NH, 128, S], BF16),
                ("tap_v", d_v, [SBn, 128, 1024], BF16), ("tap_at", d_at, [NH, 128, S], BF16),
                ("tap_xa", d_xa, [SBn, 128, D], F32), ("tap_yf", d_yf, [SBn, 128, D], F32),
                ("tap_sg", d_sg, [OSLn, 128, IC * 512], BF16),
                ("tap_cs", d_cs, [1, D], F32)]
        pb.barrier()
        for name, src, shp, dtp in taps:
            o_tap = nc.dram_tensor(name, shp, dtp, kind="ExternalOutput").ap()
            E("sync", lambda s, o=o_tap, i=src: s.dma_start(out=o, in_=i),
              inc="dout0", inc_k=16)

    E("sync", lambda s: s.nop(),
      waits=[("dout0", pb.ctr["dout0"]), ("dout1", pb.ctr["dout1"])])

    with nc.Block() as block:
        pb.flush(block)
    sc_.close()
    glob.close()
    return nc


_NC_CACHE = None


def _host_prep(L0_attn_x, L0_attn_wq, L0_attn_wk, L0_attn_wv, L0_attn_wo,
               L0_ffn_weight, L0_ffn_scale_param):
    wq = np.ascontiguousarray(
        L0_attn_wq.reshape(NH, 128, IC, 128).transpose(0, 3, 2, 1)
    ).reshape(NH, 128, D)
    wk = np.ascontiguousarray(
        L0_attn_wk.reshape(NKV, 128, IC, 128).transpose(0, 3, 2, 1)
    ).reshape(NKV, 128, D)
    wv = np.ascontiguousarray(
        L0_attn_wv.reshape(2, 512, IC, 128).transpose(0, 2, 3, 1))
    wo = np.ascontiguousarray(
        L0_attn_wo.reshape(OSLn, 512, IC, 128).transpose(0, 2, 3, 1))
    wf = np.ascontiguousarray(
        L0_ffn_weight.reshape(OSLn, 512, IC, 128).transpose(0, 2, 3, 1))
    sp = np.ascontiguousarray(L0_ffn_scale_param.reshape(1, D)).astype(np.float32)

    inv_freq = (1.0 / (500000.0 ** (np.arange(0, HD, 2, dtype=np.float32) / HD))
                ).astype(np.float32)
    t = np.arange(S, dtype=np.float32)
    freqs = np.outer(t, inv_freq).astype(np.float32)
    emb = np.concatenate([freqs, freqs], -1)
    cosT = np.ascontiguousarray(np.cos(emb).astype(np.float32).T)
    sinT = np.sin(emb).astype(np.float32).T
    sin2 = sinT.copy()
    sin2[0:64, :] *= -1.0
    sin2 = np.ascontiguousarray(sin2)
    tri = np.where(np.arange(128)[:, None] <= np.arange(128)[None, :],
                   0.0, -1.0e30).astype(np.float32)
    tri2 = np.ascontiguousarray(tri.T)
    eye = np.eye(128, dtype=np.float32)

    shared = {
        "wq_s": wq, "wk_s": wk, "wv_s": wv, "wo_s": wo, "wf_s": wf, "sp": sp,
        "ropecos": cosT, "ropesin2": sin2, "trimask": tri, "trimask2": tri2,
        "eye128": eye,
    }
    per_core = []
    for b in range(8):
        xb = np.ascontiguousarray(L0_attn_x[b])
        xt = np.ascontiguousarray(xb.T).reshape(IC, 128, S)
        m = dict(shared)
        m["xt"] = xt
        m["xn"] = xb
        per_core.append(m)
    return per_core


def kernel(L0_attn_x, L0_attn_wq, L0_attn_wk, L0_attn_wv, L0_attn_wo,
           L0_ffn_weight, L0_ffn_scale_param):
    global _NC_CACHE
    in_maps = _host_prep(np.asarray(L0_attn_x, np.float32),
                         np.asarray(L0_attn_wq, np.float32),
                         np.asarray(L0_attn_wk, np.float32),
                         np.asarray(L0_attn_wv, np.float32),
                         np.asarray(L0_attn_wo, np.float32),
                         np.asarray(L0_ffn_weight, np.float32),
                         np.asarray(L0_ffn_scale_param, np.float32))
    if _NC_CACHE is None:
        _NC_CACHE = build_nc()
    nc = _NC_CACHE
    res = run_bass_kernel_spmd(nc, in_maps, core_ids=list(range(8)))
    out = np.stack([np.asarray(r["out"], np.float32) for r in res.results], axis=0)
    return out
